# version 01 (final)
# baseline (speedup 1.0000x reference)
import numpy as np
import concourse.bass as bass
import concourse.mybir as mybir
from concourse.bass_utils import run_bass_kernel_spmd

F32 = mybir.dt.float32
BF16 = mybir.dt.bfloat16
I32 = mybir.dt.int32
AF = mybir.ActivationFunctionType
ALU = mybir.AluOpType
AX = mybir.AxisListType

NCORES = 8
B_LOC = 2
T = 2048
D = 1024
NTOK = B_LOC * T

ENGS = ("sync", "gpsimd", "scalar", "vector", "tensor")


class Buf:
    __slots__ = ("t", "w", "r", "h")

    def __init__(self, t, h=None):
        self.t = t
        self.w = None
        self.r = []
        self.h = h


class P:

    def __init__(self, nc):
        self.nc = nc
        self.ops = {e: [] for e in ENGS}
        self.cnt = {e: 0 for e in ENGS}
        self.waited = {e: {s: 0 for s in ENGS} for e in ENGS}
        self.sems = {}
        self._ctx = []
        self._name_id = 0

    def name(self, pfx):
        self._name_id += 1
        return f"{pfx}_{self._name_id}"

    def sb(self, shape, dtype, pfx="sb"):
        h = self.nc.sbuf_tensor(self.name(pfx), shape, dtype)
        t = h.__enter__()
        self._ctx.append(h)
        b = Buf(t, h)
        b.r = list(getattr(self, "pending_fence", []))
        return b

    def free(self, buf):
        toks = ([buf.w] if buf.w is not None else []) + list(buf.r)
        self.pending_fence = getattr(self, "pending_fence", []) + toks
        self._ctx.remove(buf.h)
        buf.h.__exit__(None, None, None)

    def ps(self, shape, dtype=F32, pfx="ps"):
        h = self.nc.psum_tensor(self.name(pfx), shape, dtype)
        t = h.__enter__()
        self._ctx.append(h)
        return Buf(t)

    def dram(self, name, shape, dtype, kind):
        t = self.nc.dram_tensor(name, shape, dtype, kind=kind)
        return Buf(t.ap())

    def op(self, eng, fn, reads=(), writes=(), inc=1):
        deps = []
        for b in reads:
            if b is not None and b.w is not None:
                deps.append(b.w)
        for b in writes:
            if b is None:
                continue
            if b.w is not None:
                deps.append(b.w)
            deps.extend(b.r)
        need = []
        for ds, dv in deps:
            if ds == eng and eng == "tensor":
                continue
            if dv > self.waited[eng][ds]:
                need.append((ds, dv))
                self.waited[eng][ds] = dv
        self.cnt[eng] += inc
        tok = (eng, self.cnt[eng])
        sems = self.sems

        def run(e, need=tuple(need), fn=fn, tok=tok, inc=inc):
            for ds, dv in need:
                e.wait_ge(sems[ds], dv)
            inst = fn(e)
            inst.then_inc(sems[tok[0]], inc)

        self.ops[eng].append(run)
        for b in reads:
            if b is not None:
                b.r.append(tok)
        for b in writes:
            if b is not None:
                b.w = tok
                b.r = []
        return tok

    def cc(self, val):
        if not hasattr(self, "constbuf"):
            self.constbuf = self.sb([128, 32], F32, "cc")
            self.constmap = {}
        if val not in self.constmap:
            i = len(self.constmap)
            assert i < 32
            self.constmap[val] = i
            self.op("vector", lambda e, i=i, val=val: e.memset(self.constbuf.t[:, i:i + 1], val),
                    writes=[self.constbuf])
        i = self.constmap[val]
        return self.constbuf.t[:, i:i + 1]

    def dma(self, dst_ap, src_ap, dstbuf, srcbuf, q="sync"):
        return self.op(q, lambda e: e.dma_start(out=dst_ap, in_=src_ap),
                       reads=[srcbuf], writes=[dstbuf], inc=16)

    def build(self):
        used = [e for e in ENGS if self.ops[e]]
        finals = {e: self.cnt[e] for e in used}
        for e in used:
            sems = self.sems

            def fin(eng, me=e):
                for s, v in finals.items():
                    if s != me and v > self.waited[me][s]:
                        eng.wait_ge(sems[s], v)
            self.ops[e].append(fin)
        semctx = []
        for e in used:
            h = self.nc.semaphore(self.name("sem_" + e))
            self.sems[e] = h.__enter__()
            semctx.append(h)
        with self.nc.Block() as block:
            for e in used:
                def runner(engobj, ops=self.ops[e]):
                    for f in ops:
                        f(engobj)
                getattr(block, e)(runner)
        for h in semctx:
            h.__exit__(None, None, None)
        for h in reversed(self._ctx):
            h.__exit__(None, None, None)
        return self.nc




def emit_bcast(p, dst, dst_off, src, src_ap_fn, n_total, consts, psbc):
    ones1 = consts["ones1"]
    for o in range(0, n_total, 512):
        n = min(512, n_total - o)
        p.op("tensor", lambda e, o=o, n=n: e.matmul(
            psbc.t[:, 0:n], ones1.t[:, :], src_ap_fn(o, n), start=True, stop=True),
            reads=[ones1, src], writes=[psbc])
        p.op("scalar", lambda e, o=o, n=n: e.copy(dst.t[:, dst_off + o:dst_off + o + n], psbc.t[:, 0:n]),
             reads=[psbc], writes=[dst])

def emit_ln_tile(p, xt, xn, scratch, stats):
    st = stats.t
    p.op("vector", lambda e: e.reduce_sum(st[:, 0:1], xt.t[:, :], axis=AX.X),
         reads=[xt], writes=[stats])
    p.op("scalar", lambda e: e.activation(scratch.t[:, :], xt.t[:, :], AF.Square,
                                          accum_out=st[:, 1:2]),
         reads=[xt, stats], writes=[scratch, stats])
    p.op("vector", lambda e: e.tensor_scalar(st[:, 2:3], st[:, 0:1], 1.0 / D, None, ALU.mult),
         reads=[stats], writes=[stats])
    p.op("vector", lambda e: e.tensor_scalar(st[:, 3:4], st[:, 1:2], 1.0 / D, None, ALU.mult),
         reads=[stats], writes=[stats])
    p.op("vector", lambda e: e.tensor_tensor(st[:, 4:5], st[:, 2:3], st[:, 2:3], ALU.mult),
         reads=[stats], writes=[stats])
    p.op("vector", lambda e: e.tensor_sub(st[:, 5:6], st[:, 3:4], st[:, 4:5]),
         reads=[stats], writes=[stats])
    epsap = p.cc(1e-5)
    p.op("scalar", lambda e: e.activation(st[:, 6:7], st[:, 5:6], AF.Sqrt, bias=epsap),
         reads=[stats, p.constbuf], writes=[stats])
    p.op("vector", lambda e: e.reciprocal(st[:, 7:8], st[:, 6:7]),
         reads=[stats], writes=[stats])
    p.op("vector", lambda e: e.tensor_scalar(xn.t[:, :], xt.t[:, :], st[:, 2:3], st[:, 7:8],
                                             ALU.subtract, ALU.mult),
         reads=[xt, stats], writes=[xn])


def emit_transpose(p, out_buf, out_sl, in_buf, in_ap, pst, ident):
    p.op("tensor", lambda e: e.transpose(pst.t[:, :], in_ap, ident.t[:, :]),
         reads=[in_buf, ident], writes=[pst])
    p.op("scalar", lambda e: e.copy(out_buf.t[out_sl], pst.t[:, :]),
         reads=[pst], writes=[out_buf])


def emit_ln_to_fm(p, x_hbm, res, name=""):
    ident = res["identbf"]
    xt = p.sb([128, D], F32, "ln_xt")
    xn = p.sb([128, D], F32, "ln_xn")
    xnb = p.sb([128, D], BF16, "ln_xnb")
    scratch = p.sb([128, D], F32, "ln_scr")
    stats = p.sb([128, 8], F32, "ln_st")
    pst = p.ps([128, 128], BF16, "ln_pst")
    xn_fm = res["xn_fm"]
    xn_tm = res.get("xn_tm")
    for tt in range(NTOK // 128):
        b, t0 = divmod(tt * 128, T)
        p.dma(xt.t[:, :], x_hbm.t[b, t0:t0 + 128, :], xt, x_hbm)
        emit_ln_tile(p, xt, xn, scratch, stats)
        if xn_tm is not None:
            p.op("scalar", lambda e, tt=tt: e.copy(xn_tm.t[:, tt * D:(tt + 1) * D], xn.t[:, :]),
                 reads=[xn], writes=[xn_tm])
        p.op("vector", lambda e: e.tensor_copy(xnb.t[:, :], xn.t[:, :]),
             reads=[xn], writes=[xnb])
        for k in range(8):
            emit_transpose(p, xn_fm, (slice(None), slice(k * NTOK + tt * 128, k * NTOK + tt * 128 + 128)),
                           xnb, xnb.t[:, k * 128:(k + 1) * 128], pst, ident)



def build_l3_ffn(p, x_hbm, y_hbm, consts):
    H = 3072
    identbf = consts["identbf"]
    wg = p.sb([128, 8 * H], BF16, "wg")
    wu = p.sb([128, 8 * H], BF16, "wu")
    wd = p.sb([128, 24 * D], BF16, "wd")
    stg = p.sb([128, H], I32, "stg")
    wgT = p.dram("l3f_wgT", [D, H], I32, "ExternalInput")
    wuT = p.dram("l3f_wuT", [D, H], I32, "ExternalInput")
    wdT = p.dram("l3f_wdT", [H, D], I32, "ExternalInput")
    for k in range(8):
        p.dma(stg.t[:, :], wgT.t[k * 128:(k + 1) * 128, :], stg, wgT)
        p.op("vector", lambda e, k=k: e.tensor_copy(wg.t[:, k * H:(k + 1) * H], stg.t[:, :]),
             reads=[stg], writes=[wg])
        p.dma(stg.t[:, :], wuT.t[k * 128:(k + 1) * 128, :], stg, wuT)
        p.op("vector", lambda e, k=k: e.tensor_copy(wu.t[:, k * H:(k + 1) * H], stg.t[:, :]),
             reads=[stg], writes=[wu])
    for k2 in range(24):
        p.dma(stg.t[:, 0:D], wdT.t[k2 * 128:(k2 + 1) * 128, :], stg, wdT)
        p.op("vector", lambda e, k2=k2: e.tensor_copy(wd.t[:, k2 * D:(k2 + 1) * D], stg.t[:, 0:D]),
             reads=[stg], writes=[wd])

    TC = 256
    nchunk = NTOK // TC
    xts = [p.sb([128, D], F32, "xt") for _ in range(TC // 128)]
    xn = p.sb([128, D], F32, "xn")
    xnb = p.sb([128, D], BF16, "xnb")
    scratch = p.sb([128, D], F32, "scr")
    stats = p.sb([128, 8], F32, "st")
    pst = p.ps([128, 128], BF16, "pst")
    xn_fm = p.sb([128, 8 * TC], BF16, "xnfm")
    psg = p.ps([128, TC], F32, "psg")
    psu = p.ps([128, TC], F32, "psu")
    hg = p.sb([128, TC], F32, "hg")
    hu = p.sb([128, TC], F32, "hu")
    hsb = p.sb([128, 24 * TC], BF16, "hsb")
    psy = [p.ps([128, 512], F32, "psy") for _ in range(2)]
    yout = p.sb([128, D], F32, "yout")

    SC = 1.0 / 127.0
    for c in range(nchunk):
        for i in range(TC // 128):
            tt = c * (TC // 128) + i
            b, t0 = divmod(tt * 128, T)
            xt = xts[i]
            p.dma(xt.t[:, :], x_hbm.t[b, t0:t0 + 128, :], xt, x_hbm)
            emit_ln_tile(p, xt, xn, scratch, stats)
            p.op("vector", lambda e: e.tensor_copy(xnb.t[:, :], xn.t[:, :]),
                 reads=[xn], writes=[xnb])
            for k in range(8):
                emit_transpose(p, xn_fm,
                               (slice(None), slice(k * TC + i * 128, k * TC + i * 128 + 128)),
                               xnb, xnb.t[:, k * 128:(k + 1) * 128], pst, identbf)
        for m in range(24):
            for k in range(8):
                p.op("tensor", lambda e, m=m, k=k: e.matmul(
                    psg.t[:, :], wg.t[:, k * H + m * 128:k * H + (m + 1) * 128],
                    xn_fm.t[:, k * TC:(k + 1) * TC],
                    start=(k == 0), stop=(k == 7)),
                    reads=[wg, xn_fm], writes=[psg])
            for k in range(8):
                p.op("tensor", lambda e, m=m, k=k: e.matmul(
                    psu.t[:, :], wu.t[:, k * H + m * 128:k * H + (m + 1) * 128],
                    xn_fm.t[:, k * TC:(k + 1) * TC],
                    start=(k == 0), stop=(k == 7)),
                    reads=[wu, xn_fm], writes=[psu])
            scap = p.cc(SC)
            p.op("scalar", lambda e: e.activation(hg.t[:, :], psg.t[:, :], AF.Silu, scale=scap),
                 reads=[psg, p.constbuf], writes=[hg])
            p.op("scalar", lambda e: e.activation(hu.t[:, :], psu.t[:, :], AF.Copy, scale=scap),
                 reads=[psu, p.constbuf], writes=[hu])
            p.op("vector", lambda e, m=m: e.tensor_tensor(
                hsb.t[:, m * TC:(m + 1) * TC], hg.t[:, :], hu.t[:, :], ALU.mult),
                reads=[hg, hu], writes=[hsb])
        for i in range(TC // 128):
            tt = c * (TC // 128) + i
            b, t0 = divmod(tt * 128, T)
            for nh in range(2):
                for k2 in range(24):
                    p.op("tensor", lambda e, i=i, nh=nh, k2=k2: e.matmul(
                        psy[nh].t[:, :], hsb.t[:, k2 * TC + i * 128:k2 * TC + (i + 1) * 128],
                        wd.t[:, k2 * D + nh * 512:k2 * D + (nh + 1) * 512],
                        start=(k2 == 0), stop=(k2 == 23)),
                        reads=[hsb, wd], writes=[psy[nh]])
                p.op("vector", lambda e, i=i, nh=nh: e.scalar_tensor_tensor(
                    yout.t[:, nh * 512:(nh + 1) * 512], psy[nh].t[:, :], 0.5 * SC,
                    xts[i].t[:, nh * 512:(nh + 1) * 512], ALU.mult, ALU.add),
                    reads=[psy[nh], xts[i]], writes=[yout])
            p.dma(y_hbm.t[b, t0:t0 + 128, :], yout.t[:, :], y_hbm, yout)



def build_l0_ffn(p, x_hbm, y_hbm, consts, mt_hbm, do_compose=True):
    identf = consts["identf"]
    identbf = consts["identbf"]
    H = 4096
    wa_in = {w: p.dram(f"l0f_wa_{w}", [10, 4, 1024], F32, "ExternalInput") for w in ("up", "dn")}
    wbx_in = {w: p.dram(f"l0f_wbx_{w}", [10, 4, 1024], F32, "ExternalInput") for w in ("up", "dn")}
    st_in = {w: p.dram(f"l0f_st_{w}", [4, 128, 32], F32, "ExternalInput") for w in ("up", "dn")}
    w_exp = p.dram("l0f_w_exp", [D, H], F32, "ExternalInput")
    w_con = p.dram("l0f_w_con", [H, D], F32, "ExternalInput")
    bexp_in = p.dram("l0f_bexp_fm", [128, 32], F32, "ExternalInput")
    upb_in = p.dram("l0f_upb_fm", [128, 32], F32, "ExternalInput")
    bcon_row = p.dram("l0f_bcon", [D], F32, "ExternalInput")
    dnb_row = p.dram("l0f_dnb", [D], F32, "ExternalInput")

    pstb = p.ps([128, 128], BF16, "pstb")
    psbc = p.ps([128, 512], F32, "psbc")

    if do_compose:
        Mcur = p.sb([128, 8 * 1024], F32, "Mcur")
        Mnext = p.sb([128, 8 * 1024], F32, "Mnext")
        Mtmp = p.sb([128, 512], F32, "Mtmp")
        Mbf = p.sb([128, 8 * 1024], BF16, "Mbf")
        waRep = p.sb([128, 1024], F32, "waRep")
        wbxRep = p.sb([128, 1024], F32, "wbxRep")
        wrow = p.sb([1, 2 * 1024], F32, "wrow")
        mtbuf = p.sb([128, 1024], BF16, "mtbuf")
        allocs = [Mcur, Mnext, Mtmp, Mbf, waRep, wbxRep, wrow, mtbuf]
        for w in ("up", "dn"):
            for blk in range(4):
                for r in range(8):
                    p.op("vector", lambda e, Mcur=Mcur, r=r: e.memset(
                        Mcur.t[:, r * 1024:(r + 1) * 1024], 0.0), writes=[Mcur])
                for r in range(8):
                    p.op("scalar", lambda e, Mcur=Mcur, r=r: e.copy(
                        Mcur.t[:, r * 1024 + r * 128:r * 1024 + (r + 1) * 128], identf.t[:, :]),
                        reads=[identf], writes=[Mcur])
                for s in range(9, -1, -1):
                    sg = 1 << s
                    p.dma(wrow.t[:, 0:1024], wa_in[w].t[s, blk, :][None, :], wrow, wa_in[w])
                    p.dma(wrow.t[:, 1024:2048], wbx_in[w].t[s, blk, :][None, :], wrow, wbx_in[w])
                    emit_bcast(p, waRep, 0, wrow, lambda o, n: wrow.t[0:1, o:o + n], 1024, consts, psbc)
                    emit_bcast(p, wbxRep, 0, wrow, lambda o, n: wrow.t[0:1, 1024 + o:1024 + o + n],
                               1024, consts, psbc)
                    for r in range(8):
                        def vw(buf, off=0):
                            return buf.t[:, off:off + 1024].rearrange(
                                "p (nb two sg) -> p nb two sg", two=2, sg=sg)
                        mr = vw(Mcur, r * 1024)
                        mn = vw(Mnext, r * 1024)
                        wav = vw(waRep)
                        wbv = vw(wbxRep)
                        mt = Mtmp.t[:, :].rearrange("p (nb sg) -> p nb sg", sg=sg)
                        for hf in range(2):
                            oth = 1 - hf
                            p.op("vector", lambda e, mr=mr, mn=mn, wav=wav, hf=hf: e.tensor_tensor(
                                mn[:, :, hf, :], mr[:, :, hf, :], wav[:, :, hf, :], ALU.mult),
                                reads=[Mcur, waRep], writes=[Mnext])
                            p.op("vector", lambda e, mr=mr, mt=mt, wbv=wbv, hf=hf, oth=oth: e.tensor_tensor(
                                mt, mr[:, :, oth, :], wbv[:, :, hf, :], ALU.mult),
                                reads=[Mcur, wbxRep], writes=[Mtmp])
                            p.op("vector", lambda e, mn=mn, mt=mt, hf=hf: e.tensor_tensor(
                                mn[:, :, hf, :], mn[:, :, hf, :], mt, ALU.add),
                                reads=[Mtmp, Mnext], writes=[Mnext])
                    Mcur, Mnext = Mnext, Mcur
                for r in range(8):
                    p.op("vector", lambda e, Mcur=Mcur, Mbf=Mbf, r=r: e.tensor_copy(
                        Mbf.t[:, r * 1024:(r + 1) * 1024], Mcur.t[:, r * 1024:(r + 1) * 1024]),
                        reads=[Mcur], writes=[Mbf])
                for kt in range(8):
                    for r in range(8):
                        emit_transpose(p, mtbuf, (slice(None), slice(r * 128, (r + 1) * 128)),
                                       Mbf, Mbf.t[:, r * 1024 + kt * 128:r * 1024 + (kt + 1) * 128],
                                       pstb, identbf)
                    p.dma(mt_hbm[w].t[blk, kt, :, :], mtbuf.t[:, :], mt_hbm[w], mtbuf)
        for b_ in reversed(allocs):
            p.free(b_)

    wexp_sb = p.sb([128, 8 * H], BF16, "wexp")
    stg = p.sb([128, H], F32, "l0stg")
    for k in range(8):
        p.dma(stg.t[:, :], w_exp.t[k * 128:(k + 1) * 128, :], stg, w_exp)
        p.op("vector", lambda e, k=k: e.tensor_copy(wexp_sb.t[:, k * H:(k + 1) * H], stg.t[:, :]),
             reads=[stg], writes=[wexp_sb])
    p.free(stg)
    bexp_sb = p.sb([128, 32], F32, "bexp")
    upb_sb = p.sb([128, 32], F32, "upb")
    stw = {}
    for w in ("up", "dn"):
        stw[w] = p.sb([128, 4 * 32], F32, "stw")
        for j in range(4):
            p.dma(stw[w].t[:, j * 32:(j + 1) * 32], st_in[w].t[j, :, :], stw[w], st_in[w])
    p.dma(bexp_sb.t[:, :], bexp_in.t[:, :], bexp_sb, bexp_in)
    p.dma(upb_sb.t[:, :], upb_in.t[:, :], upb_sb, upb_in)
    brow = p.sb([1, 2 * D], F32, "brow")
    bfin = p.sb([128, D], F32, "bfin")
    p.dma(brow.t[:, 0:D], bcon_row.t[None, :], brow, bcon_row)
    p.dma(brow.t[:, D:2 * D], dnb_row.t[None, :], brow, dnb_row)
    p.op("vector", lambda e: e.tensor_add(brow.t[0:1, 0:D], brow.t[0:1, 0:D], brow.t[0:1, D:2 * D]),
         reads=[brow], writes=[brow])
    emit_bcast(p, bfin, 0, brow, lambda o, n: brow.t[0:1, o:o + n], D, consts, psbc)

    TC = 256
    nchunk = NTOK // TC
    xts = [p.sb([128, D], F32, "xt") for _ in range(TC // 128)]
    xn = p.sb([128, D], F32, "xn")
    xnb = p.sb([128, D], BF16, "xnb")
    scratch = p.sb([128, D], F32, "scr")
    stats = p.sb([128, 8], F32, "st")
    xn_fm = p.sb([128, 8 * TC], BF16, "xnfm")
    v0 = p.sb([128, 32 * TC], BF16, "v0")
    v1 = p.sb([128, 32 * TC], BF16, "v1")
    v2 = p.sb([128, 32 * TC], BF16, "v2")
    mtblk = p.sb([128, 8 * 1024], BF16, "mtblk")
    wcstg = p.sb([128, D], F32, "wcstg")
    wcbf = p.sb([128, D], BF16, "wcbf")
    tmpf = p.sb([128, TC], F32, "tmpf")
    psh = p.ps([128, TC], F32, "psh")
    psy = [p.ps([128, 512], F32, "psy") for _ in range(4)]
    yout = p.sb([128, D], F32, "yout")

    def butterfly(p, src, dst, mid, w):
        for blk in range(4):
            p.dma(mtblk.t[:, :].rearrange("p (kt m) -> p kt m", kt=8),
                  mt_hbm[w].t[blk].rearrange("kt p m -> p kt m"), mtblk, mt_hbm[w])
            for m8 in range(8):
                for kt in range(8):
                    p.op("tensor", lambda e, blk=blk, m8=m8, kt=kt: e.matmul(
                        psh.t[:, :], mtblk.t[:, kt * 1024 + m8 * 128:kt * 1024 + (m8 + 1) * 128],
                        src.t[:, (blk * 8 + kt) * TC:(blk * 8 + kt + 1) * TC],
                        start=(kt == 0), stop=(kt == 7)),
                        reads=[mtblk, src], writes=[psh])
                p.op("scalar", lambda e, blk=blk, m8=m8: e.copy(
                    mid.t[:, (blk * 8 + m8) * TC:(blk * 8 + m8 + 1) * TC], psh.t[:, :]),
                    reads=[psh], writes=[mid])
        sw = stw[w]
        for stage, (off, part) in enumerate(((0, 8), (64, 16))):
            a, b = (mid, dst) if stage == 0 else (dst, mid)
            for ft in range(32):
                pf = ft ^ part
                wac = sw.t[:, off + ft:off + ft + 1]
                wbc = sw.t[:, off + 32 + ft:off + 32 + ft + 1]
                p.op("vector", lambda e, a=a, pf=pf, wbc=wbc: e.tensor_scalar(
                    tmpf.t[:, :], a.t[:, pf * TC:(pf + 1) * TC], wbc, None, ALU.mult),
                    reads=[a, sw], writes=[tmpf])
                p.op("vector", lambda e, a=a, b=b, ft=ft, wac=wac: e.scalar_tensor_tensor(
                    b.t[:, ft * TC:(ft + 1) * TC], a.t[:, ft * TC:(ft + 1) * TC], wac,
                    tmpf.t[:, :], ALU.mult, ALU.add),
                    reads=[a, sw, tmpf], writes=[b])
        return mid

    for c in range(nchunk):
        for i in range(TC // 128):
            tt = c * (TC // 128) + i
            b, t0 = divmod(tt * 128, T)
            p.dma(xts[i].t[:, :], x_hbm.t[b, t0:t0 + 128, :], xts[i], x_hbm)
            emit_ln_tile(p, xts[i], xn, scratch, stats)
            p.op("vector", lambda e: e.tensor_copy(xnb.t[:, :], xn.t[:, :]),
                 reads=[xn], writes=[xnb])
            for k in range(8):
                emit_transpose(p, xn_fm,
                               (slice(None), slice(k * TC + i * 128, k * TC + i * 128 + 128)),
                               xnb, xnb.t[:, k * 128:(k + 1) * 128], pstb, identbf)
        for m in range(32):
            for k in range(8):
                p.op("tensor", lambda e, m=m, k=k: e.matmul(
                    psh.t[:, :], wexp_sb.t[:, k * H + m * 128:k * H + (m + 1) * 128],
                    xn_fm.t[:, k * TC:(k + 1) * TC], start=(k == 0), stop=(k == 7)),
                    reads=[wexp_sb, xn_fm], writes=[psh])
            p.op("scalar", lambda e, m=m: e.activation(
                v0.t[:, m * TC:(m + 1) * TC], psh.t[:, :], AF.Identity, bias=bexp_sb.t[:, m:m + 1]),
                reads=[psh, bexp_sb], writes=[v0])
        bu = butterfly(p, v0, v2, v1, "up")
        for m in range(32):
            p.op("scalar", lambda e, bu=bu, m=m: e.activation(
                v0.t[:, m * TC:(m + 1) * TC], bu.t[:, m * TC:(m + 1) * TC], AF.Gelu,
                bias=upb_sb.t[:, m:m + 1]),
                reads=[bu, upb_sb], writes=[v0])
        bd = butterfly(p, v0, v2, v1, "dn")
        for k2 in range(32):
            p.dma(wcstg.t[:, :], w_con.t[k2 * 128:(k2 + 1) * 128, :], wcstg, w_con)
            p.op("vector", lambda e: e.tensor_copy(wcbf.t[:, :], wcstg.t[:, :]),
                 reads=[wcstg], writes=[wcbf])
            for i in range(TC // 128):
                for nh in range(2):
                    p.op("tensor", lambda e, bd=bd, k2=k2, i=i, nh=nh: e.matmul(
                        psy[i * 2 + nh].t[:, :], bd.t[:, k2 * TC + i * 128:k2 * TC + (i + 1) * 128],
                        wcbf.t[:, nh * 512:(nh + 1) * 512],
                        start=(k2 == 0), stop=(k2 == 31)),
                        reads=[bd, wcbf], writes=[psy[i * 2 + nh]])
        for i in range(TC // 128):
            tt = c * (TC // 128) + i
            b, t0 = divmod(tt * 128, T)
            for nh in range(2):
                p.op("vector", lambda e, i=i, nh=nh: e.tensor_add(
                    yout.t[:, nh * 512:(nh + 1) * 512], psy[i * 2 + nh].t[:, :],
                    bfin.t[:, nh * 512:(nh + 1) * 512]),
                    reads=[psy[i * 2 + nh], bfin], writes=[yout])
            p.op("vector", lambda e, i=i: e.scalar_tensor_tensor(
                yout.t[:, :], yout.t[:, :], 0.5, xts[i].t[:, :], ALU.mult, ALU.add),
                reads=[xts[i], yout], writes=[yout])
            p.dma(y_hbm.t[b, t0:t0 + 128, :], yout.t[:, :], y_hbm, yout)



def emit_top2(p, lg, st, comb, w1w2_from_softmax):
    s = st.t
    if w1w2_from_softmax:
        p.op("vector", lambda e: e.reduce_max(s[:, 0:1], lg.t[:, :], axis=AX.X, negate=True),
             reads=[lg], writes=[st])
        p.op("scalar", lambda e: e.activation(lg.t[:, :], lg.t[:, :], AF.Exp, bias=s[:, 0:1]),
             reads=[lg, st], writes=[lg])
        p.op("vector", lambda e: e.reduce_sum(s[:, 1:2], lg.t[:, :], axis=AX.X),
             reads=[lg], writes=[st])
        p.op("vector", lambda e: e.reciprocal(s[:, 2:3], s[:, 1:2]),
             reads=[st], writes=[st])
        p.op("vector", lambda e: e.tensor_scalar(lg.t[:, :], lg.t[:, :], s[:, 2:3], None, ALU.mult),
             reads=[lg, st], writes=[lg])
    p.op("vector", lambda e: e.reduce_max(s[:, 3:4], lg.t[:, :], axis=AX.X),
         reads=[lg], writes=[st])
    p.op("vector", lambda e: e.tensor_scalar(s[:, 8:16], lg.t[:, :], s[:, 3:4], None, ALU.is_equal),
         reads=[lg, st], writes=[st])
    p.op("vector", lambda e: e.scalar_tensor_tensor(lg.t[:, :], s[:, 8:16], -1e30, lg.t[:, :],
                                                    ALU.mult, ALU.add),
         reads=[st, lg], writes=[lg])
    p.op("vector", lambda e: e.reduce_max(s[:, 4:5], lg.t[:, :], axis=AX.X),
         reads=[lg], writes=[st])
    p.op("vector", lambda e: e.tensor_scalar(lg.t[:, :], lg.t[:, :], s[:, 4:5], None, ALU.is_equal),
         reads=[lg, st], writes=[lg])
    if w1w2_from_softmax:
        p.op("vector", lambda e: e.tensor_tensor(s[:, 5:6], s[:, 3:4], s[:, 4:5], ALU.add),
             reads=[st], writes=[st])
        p.op("vector", lambda e: e.tensor_scalar(s[:, 5:6], s[:, 5:6], 1e-20, None, ALU.add),
             reads=[st], writes=[st])
        p.op("vector", lambda e: e.reciprocal(s[:, 6:7], s[:, 5:6]),
             reads=[st], writes=[st])
        p.op("vector", lambda e: e.tensor_tensor(s[:, 0:1], s[:, 3:4], s[:, 6:7], ALU.mult),
             reads=[st], writes=[st])
        p.op("vector", lambda e: e.tensor_tensor(s[:, 1:2], s[:, 4:5], s[:, 6:7], ALU.mult),
             reads=[st], writes=[st])
    else:
        p.op("vector", lambda e: e.tensor_tensor(s[:, 5:6], s[:, 4:5], s[:, 3:4], ALU.subtract),
             reads=[st], writes=[st])
        p.op("scalar", lambda e: e.activation(s[:, 6:7], s[:, 5:6], AF.Exp),
             reads=[st], writes=[st])
        p.op("vector", lambda e: e.tensor_scalar(s[:, 7:8], s[:, 6:7], 1.0, None, ALU.add),
             reads=[st], writes=[st])
        p.op("vector", lambda e: e.reciprocal(s[:, 0:1], s[:, 7:8]),
             reads=[st], writes=[st])
        p.op("vector", lambda e: e.tensor_tensor(s[:, 1:2], s[:, 6:7], s[:, 0:1], ALU.mult),
             reads=[st], writes=[st])
    p.op("vector", lambda e: e.tensor_scalar(s[:, 8:16], s[:, 8:16], s[:, 0:1], None, ALU.mult),
         reads=[st], writes=[st])
    p.op("vector", lambda e: e.scalar_tensor_tensor(comb[0].t[comb[1]], lg.t[:, :], s[:, 1:2],
                                                    s[:, 8:16], ALU.mult, ALU.add),
         reads=[lg, st], writes=[comb[0]])


def build_moe_ffn(p, x_hbm, y_hbm, consts, which):
    identbf = consts["identbf"]
    identf = consts["identf"]
    L2 = which == "l2"
    H = 4096 if L2 else 1024
    KH = H // 128
    NE = 8
    TC = 512
    nchunk = NTOK // TC
    pfx = which + "f_"

    if L2:
        gate_w = p.dram(pfx + "gate_w", [D, 8], F32, "ExternalInput")
        gate_b = p.dram(pfx + "gate_b", [8], F32, "ExternalInput")
        w1 = p.dram(pfx + "w1", [NE, D, H], F32, "ExternalInput")
        b1r = p.dram(pfx + "b1r", [NE, 128, KH], F32, "ExternalInput")
        w2 = p.dram(pfx + "w2", [NE, H, D], F32, "ExternalInput")
        b2 = p.dram(pfx + "b2", [NE, D], F32, "ExternalInput")
    else:
        gate_w = p.dram(pfx + "gate_wT", [D, 8], F32, "ExternalInput")
        w1 = p.dram(pfx + "up_w", [NE, D, H], F32, "ExternalInput")
        b1r = p.dram(pfx + "up_br", [NE, 128, KH], F32, "ExternalInput")
        w2 = p.dram(pfx + "down_w", [NE, H, D], F32, "ExternalInput")
        b2 = p.dram(pfx + "down_b", [NE, D], F32, "ExternalInput")
        shw1 = p.dram(pfx + "sh_up_w", [D, H], F32, "ExternalInput")
        shb1r = p.dram(pfx + "sh_up_br", [128, KH], F32, "ExternalInput")
        shw2 = p.dram(pfx + "sh_down_w", [H, D], F32, "ExternalInput")
        shb2 = p.dram(pfx + "sh_down_b", [D], F32, "ExternalInput")
    acc_hbm = p.dram(pfx + "acc", [NTOK // 128, 128, D], F32, "Internal")
    xnfm_hbm = p.dram(pfx + "xnfm", [128, 8 * NTOK], BF16, "Internal") if L2 else None

    xn_fm = p.sb([128, 8 * (NTOK if not L2 else TC)], BF16, "xnfm")
    xt = p.sb([128, D], F32, "xt")
    xn = p.sb([128, D], F32, "xn")
    xnb = p.sb([128, D], BF16, "xnb")
    scratch = p.sb([128, D], F32, "scr")
    stats = p.sb([128, 16], F32, "st")
    pstb = p.ps([128, 128], BF16, "pstb")
    pstf = p.ps([128, 128], F32, "pstf")
    psg = p.ps([128, 8], F32, "psg")
    comb = p.sb([128, (NTOK // 128) * 8], F32, "comb")
    combTb = p.sb([8, NTOK], BF16, "combTb")
    lg = p.sb([128, 8], F32, "lg")
    gw_sb = p.sb([128, 8 * 8], BF16, "gwsb")
    HSTG = min(H, 1024)
    stg = p.sb([128, HSTG], F32, "stg")
    for k in range(8):
        p.dma(stg.t[:, 0:8], gate_w.t[k * 128:(k + 1) * 128, :], stg, gate_w)
        p.op("vector", lambda e, k=k: e.tensor_copy(gw_sb.t[:, k * 8:(k + 1) * 8], stg.t[:, 0:8]),
             reads=[stg], writes=[gw_sb])
    if L2:
        gbrow = p.sb([1, 8], F32, "gbrow")
        gbrep = p.sb([128, 8], F32, "gbrep")
        psbc = p.ps([128, 512], F32, "psbc")
        p.dma(gbrow.t[:, :], gate_b.t[None, :], gbrow, gate_b)
        emit_bcast(p, gbrep, 0, gbrow, lambda o, n: gbrow.t[0:1, o:o + n], 8, consts, psbc)

    for tt in range(NTOK // 128):
        b, t0 = divmod(tt * 128, T)
        c, i = divmod(tt, TC // 128)
        p.dma(xt.t[:, :], x_hbm.t[b, t0:t0 + 128, :], xt, x_hbm)
        emit_ln_tile(p, xt, xn, scratch, stats)
        p.op("vector", lambda e: e.tensor_copy(xnb.t[:, :], xn.t[:, :]),
             reads=[xn], writes=[xnb])
        for k in range(8):
            if L2:
                sl = (slice(None), slice(k * TC + i * 128, k * TC + i * 128 + 128))
            else:
                sl = (slice(None), slice(k * NTOK + tt * 128, k * NTOK + tt * 128 + 128))
            emit_transpose(p, xn_fm, sl, xnb, xnb.t[:, k * 128:(k + 1) * 128], pstb, identbf)
        if L2 and i == TC // 128 - 1:
            for k in range(8):
                p.dma(xnfm_hbm.t[:, k * NTOK + c * TC:k * NTOK + (c + 1) * TC],
                      xn_fm.t[:, k * TC:(k + 1) * TC], xnfm_hbm, xn_fm)
        for k in range(8):
            if L2:
                xsl = xn_fm.t[:, k * TC + i * 128:k * TC + i * 128 + 128]
            else:
                xsl = xn_fm.t[:, k * NTOK + tt * 128:k * NTOK + tt * 128 + 128]
            p.op("tensor", lambda e, k=k, xsl=xsl: e.matmul(
                psg.t[:, :], xsl, gw_sb.t[:, k * 8:(k + 1) * 8],
                start=(k == 0), stop=(k == 7)),
                reads=[xn_fm, gw_sb], writes=[psg])
        if L2:
            p.op("vector", lambda e: e.tensor_add(lg.t[:, :], psg.t[:, :], gbrep.t[:, :]),
                 reads=[psg, gbrep], writes=[lg])
        else:
            p.op("vector", lambda e: e.tensor_copy(lg.t[:, :], psg.t[:, :]),
                 reads=[psg], writes=[lg])
        emit_top2(p, lg, stats, (comb, (slice(None), slice(tt * 8, (tt + 1) * 8))),
                  w1w2_from_softmax=(not L2))
        p.op("tensor", lambda e, tt=tt: e.transpose(
            pstf.t[0:8, 0:128], comb.t[:, tt * 8:(tt + 1) * 8], identf.t[:, :]),
            reads=[comb, identf], writes=[pstf])
        p.op("scalar", lambda e, tt=tt: e.copy(combTb.t[:, tt * 128:(tt + 1) * 128], pstf.t[0:8, 0:128]),
             reads=[pstf], writes=[combTb])

    w1_sb = p.sb([128, 8 * H], BF16, "w1sb")
    w2_sb = p.sb([128, KH * D], BF16, "w2sb")
    b1_sb = p.sb([128, KH], F32, "b1sb")
    h_sb = p.sb([128, KH * 128 * (TC // 128)], BF16, "hsb")
    psh = p.ps([128, TC], F32, "psh")
    psy = p.ps([128, 512], F32, "psy")
    ysb = scratch
    acc_sb = p.sb([128, D], F32, "accsb")

    n_exp = NE + (0 if L2 else 1)
    for e_i in range(n_exp):
        shared = (e_i == NE)
        for k in range(8):
            for hh in range(H // HSTG):
                src = (shw1.t[k * 128:(k + 1) * 128, hh * HSTG:(hh + 1) * HSTG] if shared
                       else w1.t[e_i, k * 128:(k + 1) * 128, hh * HSTG:(hh + 1) * HSTG])
                sbuf_src = shw1 if shared else w1
                p.dma(stg.t[:, :], src, stg, sbuf_src)
                p.op("vector", lambda e, k=k, hh=hh: e.tensor_copy(
                    w1_sb.t[:, k * H + hh * HSTG:k * H + (hh + 1) * HSTG], stg.t[:, :]),
                    reads=[stg], writes=[w1_sb])
        for k2 in range(KH):
            src = shw2.t[k2 * 128:(k2 + 1) * 128, :] if shared else w2.t[e_i, k2 * 128:(k2 + 1) * 128, :]
            sbuf_src = shw2 if shared else w2
            p.dma(stg.t[:, 0:D], src, stg, sbuf_src)
            p.op("vector", lambda e, k2=k2: e.tensor_copy(w2_sb.t[:, k2 * D:(k2 + 1) * D], stg.t[:, 0:D]),
                 reads=[stg], writes=[w2_sb])
        src = shb1r.t[:, :] if shared else b1r.t[e_i, :, :]
        p.dma(b1_sb.t[:, :], src, b1_sb, shb1r if shared else b1r)
        for c in range(nchunk):
            if L2:
                for k in range(8):
                    p.dma(xn_fm.t[:, k * TC:(k + 1) * TC],
                          xnfm_hbm.t[:, k * NTOK + c * TC:k * NTOK + (c + 1) * TC],
                          xn_fm, xnfm_hbm)
            for m in range(KH):
                for k in range(8):
                    if L2:
                        xsl = xn_fm.t[:, k * TC:(k + 1) * TC]
                    else:
                        xsl = xn_fm.t[:, k * NTOK + c * TC:k * NTOK + (c + 1) * TC]
                    p.op("tensor", lambda e, m=m, k=k, xsl=xsl: e.matmul(
                        psh.t[:, :], w1_sb.t[:, k * H + m * 128:k * H + (m + 1) * 128], xsl,
                        start=(k == 0), stop=(k == 7)),
                        reads=[w1_sb, xn_fm], writes=[psh])
                actf = AF.Gelu if L2 else AF.Silu
                p.op("scalar", lambda e, m=m, actf=actf: e.activation(
                    h_sb.t[:, m * TC:(m + 1) * TC], psh.t[:, :], actf, bias=b1_sb.t[:, m:m + 1]),
                    reads=[psh, b1_sb], writes=[h_sb])
            for i in range(TC // 128):
                tt = c * (TC // 128) + i
                ce = None if shared else comb.t[:, tt * 8 + e_i:tt * 8 + e_i + 1]
                if e_i > 0:
                    p.dma(acc_sb.t[:, :], acc_hbm.t[tt, :, :], acc_sb, acc_hbm)
                for nh in range(2):
                    for k2 in range(KH):
                        p.op("tensor", lambda e, i=i, nh=nh, k2=k2: e.matmul(
                            psy.t[:, :], h_sb.t[:, k2 * TC + i * 128:k2 * TC + (i + 1) * 128],
                            w2_sb.t[:, k2 * D + nh * 512:k2 * D + (nh + 1) * 512],
                            start=(k2 == 0), stop=(k2 == KH - 1)),
                            reads=[h_sb, w2_sb], writes=[psy])
                    if shared:
                        p.op("vector", lambda e, nh=nh: e.tensor_add(
                            ysb.t[:, nh * 512:(nh + 1) * 512], psy.t[:, :],
                            acc_sb.t[:, nh * 512:(nh + 1) * 512]),
                            reads=[psy, acc_sb], writes=[ysb])
                    elif e_i == 0:
                        p.op("vector", lambda e, nh=nh, ce=ce: e.tensor_scalar(
                            ysb.t[:, nh * 512:(nh + 1) * 512], psy.t[:, :], ce, None, ALU.mult),
                            reads=[psy, comb], writes=[ysb])
                    else:
                        p.op("vector", lambda e, nh=nh, ce=ce: e.scalar_tensor_tensor(
                            ysb.t[:, nh * 512:(nh + 1) * 512], psy.t[:, :], ce,
                            acc_sb.t[:, nh * 512:(nh + 1) * 512], ALU.mult, ALU.add),
                            reads=[psy, comb, acc_sb], writes=[ysb])
                p.dma(acc_hbm.t[tt, :, :], ysb.t[:, :], acc_hbm, ysb)

    b2_sb = p.sb([8, D], F32, "b2sb")
    b2_sbb = p.sb([8, D], BF16, "b2sbb")
    psb2 = p.ps([128, 512], F32, "psb2")
    if L2:
        for e_i in range(NE):
            p.dma(b2_sb.t[e_i:e_i + 1, :], b2.t[e_i:e_i + 1, :], b2_sb, b2)
    else:
        for e_i in range(NE):
            p.dma(b2_sb.t[e_i:e_i + 1, :], b2.t[e_i:e_i + 1, :], b2_sb, b2)
        shdb_row = p.sb([1, D], F32, "shdbrow")
        shdb = p.sb([128, D], F32, "shdb")
        psbc2 = p.ps([128, 512], F32, "psbc2")
        p.dma(shdb_row.t[:, :], shb2.t[None, :], shdb_row, shb2)
        emit_bcast(p, shdb, 0, shdb_row, lambda o, n: shdb_row.t[0:1, o:o + n], D, consts, psbc2)
    p.op("vector", lambda e: e.tensor_copy(b2_sbb.t[:, :], b2_sb.t[:, :]),
         reads=[b2_sb], writes=[b2_sbb])
    t1 = xn
    for tt in range(NTOK // 128):
        b, t0 = divmod(tt * 128, T)
        p.dma(xt.t[:, :], x_hbm.t[b, t0:t0 + 128, :], xt, x_hbm)
        p.dma(acc_sb.t[:, :], acc_hbm.t[tt, :, :], acc_sb, acc_hbm)
        for nh in range(2):
            p.op("tensor", lambda e, tt=tt, nh=nh: e.matmul(
                psb2.t[:, :], combTb.t[:, tt * 128:(tt + 1) * 128],
                b2_sbb.t[:, nh * 512:(nh + 1) * 512], start=True, stop=True),
                reads=[combTb, b2_sbb], writes=[psb2])
            p.op("vector", lambda e, nh=nh: e.tensor_add(
                t1.t[:, nh * 512:(nh + 1) * 512], acc_sb.t[:, nh * 512:(nh + 1) * 512],
                psb2.t[:, :]),
                reads=[acc_sb, psb2], writes=[t1])
        if not L2:
            p.op("vector", lambda e: e.tensor_add(t1.t[:, :], t1.t[:, :], shdb.t[:, :]),
                 reads=[shdb, t1], writes=[t1])
        p.op("vector", lambda e: e.scalar_tensor_tensor(
            t1.t[:, :], t1.t[:, :], 0.5, xt.t[:, :], ALU.mult, ALU.add),
            reads=[xt, t1], writes=[t1])
        p.dma(y_hbm.t[b, t0:t0 + 128, :], t1.t[:, :], y_hbm, t1)



def build_l1_attn(p, x_hbm, y_hbm, consts):
    identbf = consts["identbf"]
    NT = T // 128
    wq = p.dram("l1a_wq", [D, D], F32, "ExternalInput")
    wk = p.dram("l1a_wk", [D, D], F32, "ExternalInput")
    wv = p.dram("l1a_wv", [D, D], F32, "ExternalInput")
    bq = p.dram("l1a_bq", [128, 8], F32, "ExternalInput")
    bk = p.dram("l1a_bk", [128, 8], F32, "ExternalInput")
    bv = p.dram("l1a_bv", [D], F32, "ExternalInput")

    pstb = p.ps([128, 128], BF16, "pstb")
    psq = p.ps([128, 512], F32, "psq")
    psr = [p.ps([128, 512], F32, "psr") for _ in range(2)]

    xt = p.sb([128, D], F32, "xt")
    xn = p.sb([128, D], F32, "xn")
    xnb = p.sb([128, D], BF16, "xnb")
    scratch = p.sb([128, D], F32, "scr")
    stats = p.sb([128, 8], F32, "st")
    xn_fm = p.sb([128, 8 * T], BF16, "xnfm")
    q_fm = p.sb([128, 8 * T], BF16, "qfm")
    k_fm = p.sb([128, 8 * T], BF16, "kfm")
    v_tm = p.sb([128, NT * D], BF16, "vtm")
    racc = p.sb([128, 4 * D], F32, "racc")
    lgt = p.sb([128, T], F32, "lgt")
    prb = p.sb([128, T], BF16, "prb")
    wstg = p.sb([128, D], F32, "wstg")
    wbf = p.sb([128, 8 * D], BF16, "wbf")
    bql = p.sb([128, 2 * 8], F32, "bql")
    bvrep = p.sb([128, D], F32, "bvrep")
    bvrow = p.sb([1, D], F32, "bvrow")
    psbc = p.ps([128, 512], F32, "psbc")
    p.dma(bql.t[:, 0:8], bq.t[:, :], bql, bq)
    p.dma(bql.t[:, 8:16], bk.t[:, :], bql, bk)
    p.dma(bvrow.t[:, :], bv.t[None, :], bvrow, bv)
    emit_bcast(p, bvrep, 0, bvrow, lambda o, n: bvrow.t[0:1, o:o + n], D, consts, psbc)
    mskdiag = p.sb([128, 128], F32, "mskdiag")
    p.op("gpsimd", lambda e: e.iota(mskdiag.t[:, :], [[1, 128]], base=0, channel_multiplier=-1,
                                    allow_small_or_imprecise_dtypes=True),
         writes=[mskdiag])
    p.op("vector", lambda e: e.tensor_single_scalar(mskdiag.t[:, :], mskdiag.t[:, :], 0.0, ALU.is_ge),
         reads=[mskdiag], writes=[mskdiag])
    p.op("vector", lambda e: e.tensor_scalar(mskdiag.t[:, :], mskdiag.t[:, :], 1e30, -1e30,
                                             ALU.mult, ALU.add),
         reads=[mskdiag], writes=[mskdiag])

    def load_w(wdr):
        for k in range(8):
            p.dma(wstg.t[:, :], wdr.t[k * 128:(k + 1) * 128, :], wstg, wdr)
            p.op("vector", lambda e, k=k: e.tensor_copy(wbf.t[:, k * D:(k + 1) * D], wstg.t[:, :]),
                 reads=[wstg], writes=[wbf])

    for b in range(B_LOC):
        for tt in range(NT):
            t0 = tt * 128
            p.dma(xt.t[:, :], x_hbm.t[b, t0:t0 + 128, :], xt, x_hbm)
            emit_ln_tile(p, xt, xn, scratch, stats)
            p.op("vector", lambda e: e.tensor_copy(xnb.t[:, :], xn.t[:, :]),
                 reads=[xn], writes=[xnb])
            for k in range(8):
                emit_transpose(p, xn_fm,
                               (slice(None), slice(k * T + t0, k * T + t0 + 128)),
                               xnb, xnb.t[:, k * 128:(k + 1) * 128], pstb, identbf)
        for wdr, out, boff in ((wq, q_fm, 0), (wk, k_fm, 8)):
            load_w(wdr)
            for m in range(8):
                for tc in range(4):
                    for k in range(8):
                        p.op("tensor", lambda e, k=k, m=m, tc=tc: e.matmul(
                            psq.t[:, :], wbf.t[:, k * D + m * 128:k * D + (m + 1) * 128],
                            xn_fm.t[:, k * T + tc * 512:k * T + (tc + 1) * 512],
                            start=(k == 0), stop=(k == 7)),
                            reads=[wbf, xn_fm], writes=[psq])
                    p.op("scalar", lambda e, m=m, tc=tc, out=out, boff=boff: e.activation(
                        out.t[:, m * T + tc * 512:m * T + (tc + 1) * 512], psq.t[:, :],
                        AF.Identity, bias=bql.t[:, boff + m:boff + m + 1]),
                        reads=[psq, bql], writes=[out])
        load_w(wv)
        for tt in range(NT):
            for nh in range(2):
                for k in range(8):
                    p.op("tensor", lambda e, k=k, tt=tt, nh=nh: e.matmul(
                        psr[nh].t[:, :], xn_fm.t[:, k * T + tt * 128:k * T + tt * 128 + 128],
                        wbf.t[:, k * D + nh * 512:k * D + (nh + 1) * 512],
                        start=(k == 0), stop=(k == 7)),
                        reads=[xn_fm, wbf], writes=[psr[nh]])
                p.op("vector", lambda e, tt=tt, nh=nh: e.tensor_add(
                    scratch.t[:, nh * 512:(nh + 1) * 512], psr[nh].t[:, :],
                    bvrep.t[:, nh * 512:(nh + 1) * 512]),
                    reads=[psr[nh], bvrep], writes=[scratch])
            p.op("vector", lambda e, tt=tt: e.tensor_copy(v_tm.t[:, tt * D:(tt + 1) * D], scratch.t[:, :]),
                 reads=[scratch], writes=[v_tm])
        for half in range(4):
            for ih in range(4):
                it = half * 4 + ih
                t0 = it * 128
                p.dma(xt.t[:, :], x_hbm.t[b, t0:t0 + 128, :], xt, x_hbm)
                emit_ln_tile(p, xt, xn, scratch, stats)
                p.op("vector", lambda e, ih=ih: e.tensor_copy(racc.t[:, ih * D:(ih + 1) * D], xn.t[:, :]),
                     reads=[xn], writes=[racc])
            for jt in range(half * 4 + 4):
                cs = (jt * 128 // 512) * 512
                for tc in range(cs // 512, 4):
                    for k in range(8):
                        p.op("tensor", lambda e, jt=jt, tc=tc, k=k: e.matmul(
                            psq.t[:, :], k_fm.t[:, k * T + jt * 128:k * T + (jt + 1) * 128],
                            q_fm.t[:, k * T + tc * 512:k * T + (tc + 1) * 512],
                            start=(k == 0), stop=(k == 7)),
                            reads=[k_fm, q_fm], writes=[psq])
                    p.op("vector", lambda e, tc=tc: e.tensor_copy(
                        lgt.t[:, tc * 512:(tc + 1) * 512], psq.t[:, :]),
                        reads=[psq], writes=[lgt])
                if jt * 128 > cs:
                    p.op("vector", lambda e, jt=jt, cs=cs: e.memset(
                        lgt.t[:, cs:jt * 128], -1e30), writes=[lgt])
                p.op("vector", lambda e, jt=jt: e.tensor_add(
                    lgt.t[:, jt * 128:(jt + 1) * 128], lgt.t[:, jt * 128:(jt + 1) * 128],
                    mskdiag.t[:, :]),
                    reads=[mskdiag, lgt], writes=[lgt])
                p.op("vector", lambda e, cs=cs: e.reduce_max(
                    stats.t[:, 0:1], lgt.t[:, cs:T], axis=AX.X, negate=True),
                    reads=[lgt], writes=[stats])
                p.op("scalar", lambda e, cs=cs: e.activation(
                    lgt.t[:, cs:T], lgt.t[:, cs:T], AF.Exp,
                    bias=stats.t[:, 0:1], accum_out=stats.t[:, 1:2]),
                    reads=[lgt, stats], writes=[lgt, stats])
                p.op("vector", lambda e: e.reciprocal(stats.t[:, 2:3], stats.t[:, 1:2]),
                     reads=[stats], writes=[stats])
                if cs > 0:
                    p.op("vector", lambda e, cs=cs: e.memset(prb.t[:, 0:cs], 0.0), writes=[prb])
                p.op("vector", lambda e, cs=cs: e.tensor_scalar(
                    prb.t[:, cs:T], lgt.t[:, cs:T], stats.t[:, 2:3],
                    1.0 / 32.0, ALU.mult, ALU.mult),
                    reads=[lgt, stats], writes=[prb])
                for ih in range(4):
                    it = half * 4 + ih
                    if it < jt:
                        continue
                    for nh in range(2):
                        p.op("tensor", lambda e, jt=jt, it=it, nh=nh: e.matmul(
                            psr[nh].t[:, :], prb.t[:, it * 128:(it + 1) * 128],
                            v_tm.t[:, jt * D + nh * 512:jt * D + (nh + 1) * 512],
                            start=True, stop=True),
                            reads=[prb, v_tm], writes=[psr[nh]])
                        p.op("vector", lambda e, ih=ih, nh=nh: e.tensor_add(
                            racc.t[:, ih * D + nh * 512:ih * D + (nh + 1) * 512],
                            racc.t[:, ih * D + nh * 512:ih * D + (nh + 1) * 512],
                            psr[nh].t[:, :]),
                            reads=[psr[nh], racc], writes=[racc])
            for ih in range(4):
                it = half * 4 + ih
                t0 = it * 128
                p.dma(xt.t[:, :], x_hbm.t[b, t0:t0 + 128, :], xt, x_hbm)
                p.op("vector", lambda e, ih=ih: e.tensor_add(
                    scratch.t[:, :], racc.t[:, ih * D:(ih + 1) * D], xt.t[:, :]),
                    reads=[racc, xt], writes=[scratch])
                p.dma(y_hbm.t[b, t0:t0 + 128, :], scratch.t[:, :], y_hbm, scratch)



def emit_softplus_row(p, buf, row_sl, ncols, tmp1, tmp2, rows=1):
    p.cc(1.0); p.cc(-1.0)
    c1 = p.constbuf.t[0:rows, p.constmap[1.0]:p.constmap[1.0] + 1]
    cm1 = p.constbuf.t[0:rows, p.constmap[-1.0]:p.constmap[-1.0] + 1]
    ap = buf.t[row_sl]
    t1 = tmp1.t[0:rows, 0:ncols]
    t2 = tmp2.t[0:rows, 0:ncols]
    p.op("scalar", lambda e: e.activation(t1, ap, AF.Abs), reads=[buf], writes=[tmp1])
    p.op("scalar", lambda e: e.activation(t1, t1, AF.Exp, scale=cm1),
         reads=[tmp1, p.constbuf], writes=[tmp1])
    p.op("scalar", lambda e: e.activation(t1, t1, AF.Ln, bias=c1),
         reads=[tmp1, p.constbuf], writes=[tmp1])
    p.op("vector", lambda e: e.tensor_single_scalar(t2, ap, 0.0, ALU.max),
         reads=[buf], writes=[tmp2])
    p.op("vector", lambda e: e.tensor_add(ap, t1, t2), reads=[tmp1, tmp2], writes=[buf])


def build_l3_attn(p, x_hbm, y_hbm, consts):
    identbf = consts["identbf"]
    NT = T // 128
    DST = 16
    ipwT = p.dram("l3a_ipwT", [D, 2 * D], F32, "ExternalInput")
    convw_fm = p.dram("l3a_convw", [128, 8 * 4], F32, "ExternalInput")
    convb_fm = p.dram("l3a_convb", [128, 8], F32, "ExternalInput")
    xpwT = p.dram("l3a_xpwT", [D, 33], F32, "ExternalInput")
    alog_r = p.dram("l3a_alog", [8, 128, DST], F32, "ExternalInput")
    d_fm = p.dram("l3a_dfm", [128, 8], F32, "ExternalInput")
    opwT = p.dram("l3a_opwT", [D, D], F32, "ExternalInput")

    pstb = p.ps([128, 128], BF16, "pstb")
    psbc = p.ps([128, 512], F32, "psbc")
    ps1 = p.ps([128, 512], F32, "ps1")
    psB = p.ps([16, 512], F32, "psB")
    psC = p.ps([16, 512], F32, "psC")
    psd = p.ps([1, 512], F32, "psd")
    pso = [p.ps([128, 512], F32, "pso") for _ in range(2)]

    xt = p.sb([128, D], F32, "xt")
    xn = p.sb([128, D], F32, "xn")
    xnb = p.sb([128, D], BF16, "xnb")
    scratch = p.sb([128, D], F32, "scr")
    stats = p.sb([128, 8], F32, "st")
    xn_fm = p.sb([128, 8 * T], BF16, "xnfm")
    xc = p.sb([128, 8 * T], BF16, "xc")
    ipw_bf = p.sb([128, 8 * D], BF16, "ipw")
    wstg = p.sb([128, D], F32, "wstg")
    xpw_bf = p.sb([128, 8 * 33], BF16, "xpw")
    convw = p.sb([128, 8 * 4], F32, "convw")
    convb = p.sb([128, 8], F32, "convb")
    dcol = p.sb([128, 8], F32, "dcol")
    A_sb = p.sb([128, 8 * DST], F32, "Asb")
    Bm = p.sb([16, T], F32, "Bm")
    Cm = p.sb([16, T], F32, "Cm")
    drow = p.sb([1, T], F32, "drow")
    brow2 = p.sb([1, 2 * T], F32, "brow2")
    dRep = p.sb([128, T], F32, "dRep")
    dBrep = p.sb([128, T], F32, "dBrep")
    CRep = p.sb([128, T], F32, "CRep")
    xsf = p.sb([128, T], F32, "xsf")
    accf = p.sb([128, T], F32, "accf")
    y_acc = p.sb([128, 1 * T], F32, "yacc")
    dA = accf
    dBu = p.sb([128, T], F32, "dBu")
    hsc = p.sb([128, T], F32, "hsc")
    opstg = xsf
    opbf = p.sb([128, D], BF16, "opbf")

    def load_ipw_half(hh):
        for k in range(8):
            p.dma(wstg.t[:, :], ipwT.t[k * 128:(k + 1) * 128, hh * D:(hh + 1) * D], wstg, ipwT)
            p.op("vector", lambda e, k=k: e.tensor_copy(ipw_bf.t[:, k * D:(k + 1) * D], wstg.t[:, :]),
                 reads=[wstg], writes=[ipw_bf])
    for k in range(8):
        p.dma(wstg.t[:, 0:33], xpwT.t[k * 128:(k + 1) * 128, :], wstg, xpwT)
        p.op("vector", lambda e, k=k: e.tensor_copy(xpw_bf.t[:, k * 33:(k + 1) * 33], wstg.t[:, 0:33]),
             reads=[wstg], writes=[xpw_bf])
        p.dma(wstg.t[:, 0:DST], alog_r.t[k, :, :], wstg, alog_r)
        p.op("scalar", lambda e, k=k: e.activation(A_sb.t[:, k * DST:(k + 1) * DST], wstg.t[:, 0:DST], AF.Exp),
             reads=[wstg], writes=[A_sb])
    p.op("vector", lambda e: e.tensor_scalar(A_sb.t[:, :], A_sb.t[:, :], -1.0, None, ALU.mult),
         reads=[A_sb], writes=[A_sb])
    p.dma(convw.t[:, :], convw_fm.t[:, :], convw, convw_fm)
    p.dma(convb.t[:, :], convb_fm.t[:, :], convb, convb_fm)
    p.dma(dcol.t[:, :], d_fm.t[:, :], dcol, d_fm)

    for b in range(B_LOC):
        for tt in range(NT):
            t0 = tt * 128
            p.dma(xt.t[:, :], x_hbm.t[b, t0:t0 + 128, :], xt, x_hbm)
            emit_ln_tile(p, xt, xn, scratch, stats)
            p.op("vector", lambda e: e.tensor_copy(xnb.t[:, :], xn.t[:, :]),
                 reads=[xn], writes=[xnb])
            for k in range(8):
                emit_transpose(p, xn_fm,
                               (slice(None), slice(k * T + t0, k * T + t0 + 128)),
                               xnb, xnb.t[:, k * 128:(k + 1) * 128], pstb, identbf)
        load_ipw_half(0)
        for f in range(8):
            for tc in range(4):
                for k in range(8):
                    p.op("tensor", lambda e, f=f, tc=tc, k=k: e.matmul(
                        ps1.t[:, :], ipw_bf.t[:, k * D + f * 128:k * D + (f + 1) * 128],
                        xn_fm.t[:, k * T + tc * 512:k * T + (tc + 1) * 512],
                        start=(k == 0), stop=(k == 7)),
                        reads=[ipw_bf, xn_fm], writes=[ps1])
                p.op("vector", lambda e, tc=tc: e.tensor_copy(xsf.t[:, tc * 512:(tc + 1) * 512], ps1.t[:, :]),
                     reads=[ps1], writes=[xsf])
            p.op("vector", lambda e, f=f: e.tensor_scalar(
                accf.t[:, :], xsf.t[:, :], convw.t[:, f * 4 + 3:f * 4 + 4], None, ALU.mult),
                reads=[xsf, convw], writes=[accf])
            for dl in (1, 2, 3):
                p.op("vector", lambda e, f=f, dl=dl: e.scalar_tensor_tensor(
                    accf.t[:, dl:T], xsf.t[:, 0:T - dl], convw.t[:, f * 4 + 3 - dl:f * 4 + 4 - dl],
                    accf.t[:, dl:T], ALU.mult, ALU.add),
                    reads=[xsf, convw, accf], writes=[accf])
            p.op("scalar", lambda e, f=f: e.activation(
                xc.t[:, f * T:(f + 1) * T], accf.t[:, :], AF.Silu, bias=convb.t[:, f:f + 1]),
                reads=[accf, convb], writes=[xc])
        for tc in range(4):
            for f in range(8):
                p.op("tensor", lambda e, f=f, tc=tc: e.matmul(
                    psB.t[:, :], xpw_bf.t[:, f * 33:f * 33 + 16],
                    xc.t[:, f * T + tc * 512:f * T + (tc + 1) * 512],
                    start=(f == 0), stop=(f == 7)),
                    reads=[xpw_bf, xc], writes=[psB])
                p.op("tensor", lambda e, f=f, tc=tc: e.matmul(
                    psC.t[:, :], xpw_bf.t[:, f * 33 + 16:f * 33 + 32],
                    xc.t[:, f * T + tc * 512:f * T + (tc + 1) * 512],
                    start=(f == 0), stop=(f == 7)),
                    reads=[xpw_bf, xc], writes=[psC])
                p.op("tensor", lambda e, f=f, tc=tc: e.matmul(
                    psd.t[:, :], xpw_bf.t[:, f * 33 + 32:f * 33 + 33],
                    xc.t[:, f * T + tc * 512:f * T + (tc + 1) * 512],
                    start=(f == 0), stop=(f == 7)),
                    reads=[xpw_bf, xc], writes=[psd])
            p.op("vector", lambda e, tc=tc: e.tensor_copy(Bm.t[:, tc * 512:(tc + 1) * 512], psB.t[:, :]),
                 reads=[psB], writes=[Bm])
            p.op("vector", lambda e, tc=tc: e.tensor_copy(Cm.t[:, tc * 512:(tc + 1) * 512], psC.t[:, :]),
                 reads=[psC], writes=[Cm])
            p.op("vector", lambda e, tc=tc: e.tensor_copy(drow.t[:, tc * 512:(tc + 1) * 512], psd.t[:, :]),
                 reads=[psd], writes=[drow])
        emit_softplus_row(p, drow, (slice(0, 1), slice(0, T)), T, xsf, accf)
        emit_bcast(p, dRep, 0, drow, lambda o, n: drow.t[0:1, o:o + n], T, consts, psbc)
        p.op("vector", lambda e: e.tensor_tensor(Bm.t[:, :], Bm.t[:, :], dRep.t[0:16, :], ALU.mult),
             reads=[dRep, Bm], writes=[Bm])
        for fg in range(8):
            fs = [fg]
            for n in range(DST):
                p.dma(brow2.t[0:1, 0:T], Bm.t[n:n + 1, :], brow2, Bm)
                p.dma(brow2.t[0:1, T:2 * T], Cm.t[n:n + 1, :], brow2, Cm)
                emit_bcast(p, dBrep, 0, brow2, lambda o, nn: brow2.t[0:1, o:o + nn], T, consts, psbc)
                emit_bcast(p, CRep, 0, brow2, lambda o, nn: brow2.t[0:1, T + o:T + o + nn], T, consts, psbc)
                for fi, f in enumerate(fs):
                    ya = y_acc.t[:, fi * T:(fi + 1) * T]
                    p.op("scalar", lambda e, f=f, n=n: e.activation(
                        dA.t[:, :], dRep.t[:, :], AF.Exp, scale=A_sb.t[:, f * DST + n:f * DST + n + 1]),
                        reads=[dRep, A_sb], writes=[dA])
                    p.op("vector", lambda e, f=f: e.tensor_tensor(
                        dBu.t[:, :], xc.t[:, f * T:(f + 1) * T], dBrep.t[:, :], ALU.mult),
                        reads=[xc, dBrep], writes=[dBu])
                    p.op("vector", lambda e: e.tensor_tensor_scan(
                        hsc.t[:, :], dA.t[:, :], dBu.t[:, :], 0.0, ALU.mult, ALU.add),
                        reads=[dA, dBu], writes=[hsc])
                    p.op("vector", lambda e: e.tensor_tensor(dA.t[:, :], hsc.t[:, :], CRep.t[:, :], ALU.mult),
                         reads=[hsc, CRep], writes=[dA])
                    if n == 0:
                        p.op("vector", lambda e, ya=ya: e.tensor_copy(ya, dA.t[:, :]),
                             reads=[dA], writes=[y_acc])
                    else:
                        p.op("vector", lambda e, ya=ya: e.tensor_tensor(ya, ya, dA.t[:, :], ALU.add),
                             reads=[dA, y_acc], writes=[y_acc])
            for fi, f in enumerate(fs):
                ya = y_acc.t[:, fi * T:(fi + 1) * T]
                p.op("vector", lambda e, f=f, ya=ya: e.scalar_tensor_tensor(
                    ya, xc.t[:, f * T:(f + 1) * T], dcol.t[:, f:f + 1], ya, ALU.mult, ALU.add),
                    reads=[xc, dcol, y_acc], writes=[y_acc])
                p.op("vector", lambda e, f=f, ya=ya: e.tensor_copy(xc.t[:, f * T:(f + 1) * T], ya),
                     reads=[y_acc], writes=[xc])
        load_ipw_half(1)
        for f in range(8):
            for tc in range(4):
                for k in range(8):
                    p.op("tensor", lambda e, f=f, tc=tc, k=k: e.matmul(
                        ps1.t[:, :], ipw_bf.t[:, k * D + f * 128:k * D + (f + 1) * 128],
                        xn_fm.t[:, k * T + tc * 512:k * T + (tc + 1) * 512],
                        start=(k == 0), stop=(k == 7)),
                        reads=[ipw_bf, xn_fm], writes=[ps1])
                p.op("scalar", lambda e, tc=tc: e.activation(
                    xsf.t[:, tc * 512:(tc + 1) * 512], ps1.t[:, :], AF.Silu),
                    reads=[ps1], writes=[xsf])
            p.op("vector", lambda e, f=f: e.tensor_tensor(
                xc.t[:, f * T:(f + 1) * T], xc.t[:, f * T:(f + 1) * T], xsf.t[:, :], ALU.mult),
                reads=[xsf, xc], writes=[xc])
        for tt in range(NT):
            t0 = tt * 128
            for k in range(8):
                p.dma(opstg.t[:, 0:D], opwT.t[k * 128:(k + 1) * 128, :], opstg, opwT)
                p.op("vector", lambda e: e.tensor_copy(opbf.t[:, :], opstg.t[:, 0:D]),
                     reads=[opstg], writes=[opbf])
                for nh in range(2):
                    p.op("tensor", lambda e, tt=tt, k=k, nh=nh: e.matmul(
                        pso[nh].t[:, :], xc.t[:, k * T + tt * 128:k * T + tt * 128 + 128],
                        opbf.t[:, nh * 512:(nh + 1) * 512],
                        start=(k == 0), stop=(k == 7)),
                        reads=[xc, opbf], writes=[pso[nh]])
            p.dma(xt.t[:, :], x_hbm.t[b, t0:t0 + 128, :], xt, x_hbm)
            for nh in range(2):
                p.op("vector", lambda e, nh=nh: e.tensor_add(
                    scratch.t[:, nh * 512:(nh + 1) * 512], pso[nh].t[:, :],
                    xt.t[:, nh * 512:(nh + 1) * 512]),
                    reads=[pso[nh], xt], writes=[scratch])
            p.dma(y_hbm.t[b, t0:t0 + 128, :], scratch.t[:, :], y_hbm, scratch)


def launch_l3_attn(x_np, ins):
    def bld():
        p = P(_mk_nc())
        x_hbm = p.dram("x", [B_LOC, T, D], F32, "ExternalInput")
        y_hbm = p.dram("y", [B_LOC, T, D], F32, "ExternalOutput")
        consts = make_consts(p)
        build_l3_attn(p, x_hbm, y_hbm, consts)
        return p.build()
    nc = _cached("l3attn", bld)
    io = {
        "l3a_ipwT": np.asarray(ins["L3_attn_in_proj_w"]).T,
        "l3a_convw": np.asarray(ins["L3_attn_conv_w"]).reshape(8, 128, 4).transpose(1, 0, 2).reshape(128, 32),
        "l3a_convb": np.asarray(ins["L3_attn_conv_b"]).reshape(8, 128).T,
        "l3a_xpwT": np.asarray(ins["L3_attn_x_proj_w"]).T,
        "l3a_alog": np.asarray(ins["L3_attn_A_log"]).reshape(8, 128, 16),
        "l3a_dfm": np.asarray(ins["L3_attn_D_param"]).reshape(8, 128).T,
        "l3a_opwT": np.asarray(ins["L3_attn_out_proj_w"]).T,
    }
    io = {k: np.ascontiguousarray(v, dtype=np.float32) for k, v in io.items()}
    eye = _np_bf16_eye()
    in_maps = [dict(io, x=x_np[c], eye128=eye) for c in range(NCORES)]
    res = run_bass_kernel_spmd(nc, in_maps, core_ids=list(range(NCORES)))
    return [r["y"] for r in res.results]



def build_l0_attn(p, x_hbm, y_hbm, consts, u_hbm):
    identbf = consts["identbf"]
    NT = T // 128
    DST = 16
    DI = 2048
    win = p.dram("l0a_win", [D, 2 * DI], F32, "ExternalInput")
    convw_fm = p.dram("l0a_convw", [128, 16 * 4], F32, "ExternalInput")
    convb_fm = p.dram("l0a_convb", [128, 16], F32, "ExternalInput")
    xpw = p.dram("l0a_xpw", [DI, 96], F32, "ExternalInput")
    wdt = p.dram("l0a_wdt", [64, DI], F32, "ExternalInput")
    bdt_fm = p.dram("l0a_bdt", [128, 16], F32, "ExternalInput")
    alog_r = p.dram("l0a_alog", [16, 128, DST], F32, "ExternalInput")
    dskip_fm = p.dram("l0a_dskip", [128, 16], F32, "ExternalInput")
    wout = p.dram("l0a_wout", [DI, D], F32, "ExternalInput")

    pstb = p.ps([128, 128], BF16, "pstb")
    psbc = p.ps([128, 512], F32, "psbc")
    ps1 = p.ps([128, 512], F32, "ps1")
    psdt = p.ps([64, 512], F32, "psdt")
    psB = p.ps([16, 512], F32, "psB")
    psC = p.ps([16, 512], F32, "psC")
    pso = [p.ps([128, 512], F32, "pso") for _ in range(2)]

    xt = p.sb([128, D], F32, "xt")
    xn = p.sb([128, D], F32, "xn")
    xnb = p.sb([128, D], BF16, "xnb")
    scratch = p.sb([128, D], F32, "scr")
    stats = p.sb([128, 8], F32, "st")
    xn_fm = p.sb([128, 8 * T], BF16, "xnfm")
    win_bf = p.sb([128, 8 * DI], BF16, "winbf")
    wstg = p.sb([128, DI], F32, "wstg")
    xpw_bf = p.sb([128, 16 * 96], BF16, "xpwbf")
    wdt_bf = p.sb([64, DI], BF16, "wdtbf")
    convw = p.sb([128, 16 * 4], F32, "convw")
    cols = p.sb([128, 3 * 16], F32, "cols")
    A_sb = p.sb([128, 16 * DST], F32, "Asb")
    dt_sb = p.sb([64, T], F32, "dtsb")
    dt_bf = p.sb([64, T], BF16, "dtbf")
    B_sb = p.sb([16, T], F32, "Bsb")
    C_sb = p.sb([16, T], F32, "Csb")
    brow = p.sb([1, T], F32, "brow")
    xsf = p.sb([128, T], F32, "xsf")
    accf = p.sb([128, T], F32, "accf")
    dA = accf
    ubf = p.sb([128, T], BF16, "ubf")
    delta = p.sb([128, T], F32, "delta")
    du = xsf
    y_acc = p.sb([128, T], F32, "yacc")
    dBu = p.sb([128, T], F32, "dBu")
    hsc = p.sb([128, T], F32, "hsc")
    dBrep = p.sb([128, T], F32, "dBrep")
    CRep = p.sb([128, T], F32, "CRep")
    wobf = p.sb([128, D], BF16, "wobf")
    ytile = p.sb([128, 128], BF16, "ytile")

    def load_win_half(hh):
        for k in range(8):
            p.dma(wstg.t[:, :], win.t[k * 128:(k + 1) * 128, hh * DI:(hh + 1) * DI], wstg, win)
            p.op("vector", lambda e, k=k: e.tensor_copy(win_bf.t[:, k * DI:(k + 1) * DI], wstg.t[:, :]),
                 reads=[wstg], writes=[win_bf])

    for k in range(16):
        p.dma(wstg.t[:, 0:96], xpw.t[k * 128:(k + 1) * 128, :], wstg, xpw)
        p.op("vector", lambda e, k=k: e.tensor_copy(xpw_bf.t[:, k * 96:(k + 1) * 96], wstg.t[:, 0:96]),
             reads=[wstg], writes=[xpw_bf])
        p.dma(wstg.t[:, 0:DST], alog_r.t[k, :, :], wstg, alog_r)
        p.op("scalar", lambda e, k=k: e.activation(A_sb.t[:, k * DST:(k + 1) * DST], wstg.t[:, 0:DST], AF.Exp),
             reads=[wstg], writes=[A_sb])
    p.op("vector", lambda e: e.tensor_scalar(A_sb.t[:, :], A_sb.t[:, :], -1.0, None, ALU.mult),
         reads=[A_sb], writes=[A_sb])
    p.dma(wstg.t[0:64, 0:DI], wdt.t[:, :], wstg, wdt)
    p.op("vector", lambda e: e.tensor_copy(wdt_bf.t[:, :], wstg.t[0:64, 0:DI]),
         reads=[wstg], writes=[wdt_bf])
    p.dma(convw.t[:, :], convw_fm.t[:, :], convw, convw_fm)
    p.dma(cols.t[:, 0:16], convb_fm.t[:, :], cols, convb_fm)
    p.dma(cols.t[:, 16:32], bdt_fm.t[:, :], cols, bdt_fm)
    p.dma(cols.t[:, 32:48], dskip_fm.t[:, :], cols, dskip_fm)

    for b in range(B_LOC):
        for tt in range(NT):
            t0 = tt * 128
            p.dma(xt.t[:, :], x_hbm.t[b, t0:t0 + 128, :], xt, x_hbm)
            emit_ln_tile(p, xt, xn, scratch, stats)
            p.op("vector", lambda e: e.tensor_copy(xnb.t[:, :], xn.t[:, :]),
                 reads=[xn], writes=[xnb])
            for k in range(8):
                emit_transpose(p, xn_fm,
                               (slice(None), slice(k * T + t0, k * T + t0 + 128)),
                               xnb, xnb.t[:, k * 128:(k + 1) * 128], pstb, identbf)
        load_win_half(0)
        for f in range(16):
            for tc in range(4):
                for k in range(8):
                    p.op("tensor", lambda e, f=f, tc=tc, k=k: e.matmul(
                        ps1.t[:, :], win_bf.t[:, k * DI + f * 128:k * DI + (f + 1) * 128],
                        xn_fm.t[:, k * T + tc * 512:k * T + (tc + 1) * 512],
                        start=(k == 0), stop=(k == 7)),
                        reads=[win_bf, xn_fm], writes=[ps1])
                p.op("vector", lambda e, tc=tc: e.tensor_copy(xsf.t[:, tc * 512:(tc + 1) * 512], ps1.t[:, :]),
                     reads=[ps1], writes=[xsf])
            p.op("vector", lambda e, f=f: e.tensor_scalar(
                accf.t[:, :], xsf.t[:, :], convw.t[:, f * 4 + 3:f * 4 + 4], None, ALU.mult),
                reads=[xsf, convw], writes=[accf])
            for dl in (1, 2, 3):
                p.op("vector", lambda e, f=f, dl=dl: e.scalar_tensor_tensor(
                    accf.t[:, dl:T], xsf.t[:, 0:T - dl], convw.t[:, f * 4 + 3 - dl:f * 4 + 4 - dl],
                    accf.t[:, dl:T], ALU.mult, ALU.add),
                    reads=[xsf, convw, accf], writes=[accf])
            p.op("scalar", lambda e, f=f: e.activation(
                ubf.t[:, :], accf.t[:, :], AF.Silu, bias=cols.t[:, f:f + 1]),
                reads=[accf, cols], writes=[ubf])
            p.dma(u_hbm.t[f, :, :], ubf.t[:, :], u_hbm, ubf)
            for tc in range(4):
                for (pst, c0, c1_) in ((psdt, 0, 64), (psB, 64, 80), (psC, 80, 96)):
                    p.op("tensor", lambda e, f=f, tc=tc, pst=pst, c0=c0, c1_=c1_: e.matmul(
                        pst.t[:, :], xpw_bf.t[:, f * 96 + c0:f * 96 + c1_],
                        ubf.t[:, tc * 512:(tc + 1) * 512], start=True, stop=True),
                        reads=[xpw_bf, ubf], writes=[pst])
                for (pst, dst, npart) in ((psdt, dt_sb, 64), (psB, B_sb, 16), (psC, C_sb, 16)):
                    if f == 0:
                        p.op("vector", lambda e, tc=tc, pst=pst, dst=dst, npart=npart: e.tensor_copy(
                            dst.t[0:npart, tc * 512:(tc + 1) * 512], pst.t[0:npart, :]),
                            reads=[pst], writes=[dst])
                    else:
                        p.op("vector", lambda e, tc=tc, pst=pst, dst=dst, npart=npart: e.tensor_add(
                            dst.t[0:npart, tc * 512:(tc + 1) * 512],
                            dst.t[0:npart, tc * 512:(tc + 1) * 512], pst.t[0:npart, :]),
                            reads=[pst, dst], writes=[dst])
        p.op("vector", lambda e: e.tensor_copy(dt_bf.t[:, :], dt_sb.t[:, :]),
             reads=[dt_sb], writes=[dt_bf])
        for f in range(16):
            for tc in range(4):
                p.op("tensor", lambda e, f=f, tc=tc: e.matmul(
                    ps1.t[:, :], wdt_bf.t[:, f * 128:(f + 1) * 128],
                    dt_bf.t[:, tc * 512:(tc + 1) * 512], start=True, stop=True),
                    reads=[wdt_bf, dt_bf], writes=[ps1])
                p.op("scalar", lambda e, f=f, tc=tc: e.activation(
                    delta.t[:, tc * 512:(tc + 1) * 512], ps1.t[:, :], AF.Identity,
                    bias=cols.t[:, 16 + f:17 + f]),
                    reads=[ps1, cols], writes=[delta])
            emit_softplus_row(p, delta, (slice(None), slice(0, T)), T, xsf, accf, rows=128)
            p.dma(ubf.t[:, :], u_hbm.t[f, :, :], ubf, u_hbm)
            p.op("vector", lambda e: e.tensor_tensor(du.t[:, :], delta.t[:, :], ubf.t[:, :], ALU.mult),
                 reads=[delta, ubf], writes=[du])
            for n in range(DST):
                p.dma(brow.t[0:1, :], B_sb.t[n:n + 1, :], brow, B_sb)
                emit_bcast(p, dBrep, 0, brow, lambda o, nn: brow.t[0:1, o:o + nn], T, consts, psbc)
                p.dma(brow.t[0:1, :], C_sb.t[n:n + 1, :], brow, C_sb)
                emit_bcast(p, CRep, 0, brow, lambda o, nn: brow.t[0:1, o:o + nn], T, consts, psbc)
                p.op("scalar", lambda e, f=f, n=n: e.activation(
                    dA.t[:, :], delta.t[:, :], AF.Exp, scale=A_sb.t[:, f * DST + n:f * DST + n + 1]),
                    reads=[delta, A_sb], writes=[dA])
                p.op("vector", lambda e: e.tensor_tensor(dBu.t[:, :], du.t[:, :], dBrep.t[:, :], ALU.mult),
                     reads=[du, dBrep], writes=[dBu])
                p.op("vector", lambda e: e.tensor_tensor_scan(
                    hsc.t[:, :], dA.t[:, :], dBu.t[:, :], 0.0, ALU.mult, ALU.add),
                    reads=[dA, dBu], writes=[hsc])
                p.op("vector", lambda e: e.tensor_tensor(hsc.t[:, :], hsc.t[:, :], CRep.t[:, :], ALU.mult),
                     reads=[CRep, hsc], writes=[hsc])
                if n == 0:
                    p.op("vector", lambda e: e.tensor_copy(y_acc.t[:, :], hsc.t[:, :]),
                         reads=[hsc], writes=[y_acc])
                else:
                    p.op("vector", lambda e: e.tensor_add(y_acc.t[:, :], y_acc.t[:, :], hsc.t[:, :]),
                         reads=[hsc, y_acc], writes=[y_acc])
            p.op("vector", lambda e, f=f: e.scalar_tensor_tensor(
                y_acc.t[:, :], ubf.t[:, :], cols.t[:, 32 + f:33 + f], y_acc.t[:, :], ALU.mult, ALU.add),
                reads=[ubf, cols, y_acc], writes=[y_acc])
            p.op("vector", lambda e: e.tensor_copy(ubf.t[:, :], y_acc.t[:, :]),
                 reads=[y_acc], writes=[ubf])
            p.dma(u_hbm.t[f, :, :], ubf.t[:, :], u_hbm, ubf)
        load_win_half(1)
        for f in range(16):
            for tc in range(4):
                for k in range(8):
                    p.op("tensor", lambda e, f=f, tc=tc, k=k: e.matmul(
                        ps1.t[:, :], win_bf.t[:, k * DI + f * 128:k * DI + (f + 1) * 128],
                        xn_fm.t[:, k * T + tc * 512:k * T + (tc + 1) * 512],
                        start=(k == 0), stop=(k == 7)),
                        reads=[win_bf, xn_fm], writes=[ps1])
                p.op("scalar", lambda e, tc=tc: e.activation(
                    xsf.t[:, tc * 512:(tc + 1) * 512], ps1.t[:, :], AF.Silu),
                    reads=[ps1], writes=[xsf])
            p.dma(ubf.t[:, :], u_hbm.t[f, :, :], ubf, u_hbm)
            p.op("vector", lambda e: e.tensor_tensor(ubf.t[:, :], ubf.t[:, :], xsf.t[:, :], ALU.mult),
                 reads=[xsf, ubf], writes=[ubf])
            p.dma(u_hbm.t[f, :, :], ubf.t[:, :], u_hbm, ubf)
        for tt in range(NT):
            t0 = tt * 128
            for k in range(16):
                p.dma(ytile.t[:, :], u_hbm.t[k, :, t0:t0 + 128], ytile, u_hbm)
                p.dma(wstg.t[:, 0:D], wout.t[k * 128:(k + 1) * 128, :], wstg, wout)
                p.op("vector", lambda e: e.tensor_copy(wobf.t[:, :], wstg.t[:, 0:D]),
                     reads=[wstg], writes=[wobf])
                for nh in range(2):
                    p.op("tensor", lambda e, k=k, nh=nh: e.matmul(
                        pso[nh].t[:, :], ytile.t[:, :], wobf.t[:, nh * 512:(nh + 1) * 512],
                        start=(k == 0), stop=(k == 15)),
                        reads=[ytile, wobf], writes=[pso[nh]])
            p.dma(xt.t[:, :], x_hbm.t[b, t0:t0 + 128, :], xt, x_hbm)
            for nh in range(2):
                p.op("vector", lambda e, nh=nh: e.tensor_add(
                    scratch.t[:, nh * 512:(nh + 1) * 512], pso[nh].t[:, :],
                    xt.t[:, nh * 512:(nh + 1) * 512]),
                    reads=[pso[nh], xt], writes=[scratch])
            p.dma(y_hbm.t[b, t0:t0 + 128, :], scratch.t[:, :], y_hbm, scratch)


def launch_l0_attn(x_np, ins):
    def bld():
        p = P(_mk_nc())
        x_hbm = p.dram("x", [B_LOC, T, D], F32, "ExternalInput")
        y_hbm = p.dram("y", [B_LOC, T, D], F32, "ExternalOutput")
        consts = make_consts(p)
        u_hbm = p.dram("l0a_u", [16, 128, T], BF16, "Internal")
        build_l0_attn(p, x_hbm, y_hbm, consts, u_hbm)
        return p.build()
    nc = _cached("l0attn", bld)
    io = {
        "l0a_win": ins["L0_attn_W_in"],
        "l0a_convw": np.asarray(ins["L0_attn_conv_w"]).reshape(16, 128, 4).transpose(1, 0, 2).reshape(128, 64),
        "l0a_convb": np.asarray(ins["L0_attn_conv_b"]).reshape(16, 128).T,
        "l0a_xpw": ins["L0_attn_W_xproj"],
        "l0a_wdt": ins["L0_attn_W_dt"],
        "l0a_bdt": np.asarray(ins["L0_attn_b_dt"]).reshape(16, 128).T,
        "l0a_alog": np.asarray(ins["L0_attn_A_log"]).reshape(16, 128, 16),
        "l0a_dskip": np.asarray(ins["L0_attn_D_skip"]).reshape(16, 128).T,
        "l0a_wout": ins["L0_attn_W_out"],
    }
    io = {k: np.ascontiguousarray(v, dtype=np.float32) for k, v in io.items()}
    eye = _np_bf16_eye()
    in_maps = [dict(io, x=x_np[c], eye128=eye) for c in range(NCORES)]
    res = run_bass_kernel_spmd(nc, in_maps, core_ids=list(range(NCORES)))
    return [r["y"] for r in res.results]



def build_l2_attn(p, x_hbm, y_hbm, consts, ctx_hbm):
    identbf = consts["identbf"]
    identf = consts["identf"]
    NT = T // 128
    NH = 16
    qkwT = p.dram("l2a_qkwT", [D, D], F32, "ExternalInput")
    qkb_fm = p.dram("l2a_qkb_fm", [128, 8], F32, "ExternalInput")
    vwT = p.dram("l2a_vwT", [D, D], F32, "ExternalInput")
    vb_row = p.dram("l2a_vb", [D], F32, "ExternalInput")
    arow_in = p.dram("l2a_arow", [D], F32, "ExternalInput")
    a6465 = p.dram("l2a_a6465", [128, 2], F32, "ExternalInput")

    pstb = p.ps([128, 128], BF16, "pstb")
    pstf = p.ps([128, 128], F32, "pstf")
    psbc = p.ps([128, 512], F32, "psbc")
    psq = [p.ps([128, 512], F32, "psq") for _ in range(2)]
    pss = p.ps([1, 512], F32, "pss")
    psctx = p.ps([128, 64], F32, "psctx")

    xt = p.sb([128, D], F32, "xt")
    xn = p.sb([128, D], F32, "xn")
    xnb = p.sb([128, D], BF16, "xnb")
    scratch = p.sb([128, D], F32, "scr")
    stats = p.sb([128, 16], F32, "st")
    xn_fm = p.sb([128, 8 * T], BF16, "xnfm")
    qk_fm = p.sb([128, 8 * T], BF16, "qkfm")
    v_tm = p.sb([128, NT * D], BF16, "vtm")
    qpack = p.sb([128, T], BF16, "qpack")
    kpack = p.sb([128, T], BF16, "kpack")
    wstg = p.sb([128, D], F32, "wstg")
    wbf = p.sb([128, 8 * D], BF16, "wbf")
    e_sb = xn_fm
    qkb_sb = p.sb([128, 8], F32, "qkb")
    bvrep = p.sb([128, D], F32, "bvrep")
    arep = p.sb([128, D], F32, "arep")
    a65_sb = p.sb([128, 2], F32, "a65")
    row = p.sb([1, 2 * T], F32, "rowf")
    rowb = p.sb([1, T], BF16, "rowb")
    onesb = p.sb([128, 1], BF16, "onesb")
    norms2 = p.sb([128, NT * NH], F32, "n2")
    qka = p.sb([128, NT * NH], F32, "qka")
    sqpm = p.sb([128, NT * NH], F32, "sqpm")
    skpm = p.sb([128, NT * NH], F32, "skpm")
    mx = p.sb([128, NH], F32, "mx")
    hrow = p.sb([16, 2 * T], F32, "hrow")
    srep = p.sb([128, 2 * NH], F32, "srep")
    mncol = p.sb([16, 4], F32, "mncol")
    rrep = p.sb([128, 1024], F32, "rrep")
    cbuf = p.sb([128, 64], F32, "cbuf")

    p.dma(qkb_sb.t[:, :], qkb_fm.t[:, :], qkb_sb, qkb_fm)
    p.dma(row.t[0:1, 0:D], vb_row.t[None, :], row, vb_row)
    emit_bcast(p, bvrep, 0, row, lambda o, n: row.t[0:1, o:o + n], D, consts, psbc)
    p.dma(row.t[0:1, 0:D], arow_in.t[None, :], row, arow_in)
    emit_bcast(p, arep, 0, row, lambda o, n: row.t[0:1, o:o + n], D, consts, psbc)
    p.dma(a65_sb.t[:, :], a6465.t[:, :], a65_sb, a6465)
    p.op("vector", lambda e: e.memset(onesb.t[:, :], 1.0), writes=[onesb])

    def load_w(wdr):
        for k in range(8):
            p.dma(wstg.t[:, :], wdr.t[k * 128:(k + 1) * 128, :], wstg, wdr)
            p.op("vector", lambda e, k=k: e.tensor_copy(wbf.t[:, k * D:(k + 1) * D], wstg.t[:, :]),
                 reads=[wstg], writes=[wbf])

    for b in range(B_LOC):
        for tt in range(NT):
            t0 = tt * 128
            p.dma(xt.t[:, :], x_hbm.t[b, t0:t0 + 128, :], xt, x_hbm)
            emit_ln_tile(p, xt, xn, scratch, stats)
            p.op("vector", lambda e: e.tensor_copy(xnb.t[:, :], xn.t[:, :]),
                 reads=[xn], writes=[xnb])
            for k in range(8):
                emit_transpose(p, xn_fm,
                               (slice(None), slice(k * T + t0, k * T + t0 + 128)),
                               xnb, xnb.t[:, k * 128:(k + 1) * 128], pstb, identbf)
        load_w(qkwT)
        for m in range(8):
            for tc in range(4):
                for k in range(8):
                    p.op("tensor", lambda e, k=k, m=m, tc=tc: e.matmul(
                        psq[0].t[:, :], wbf.t[:, k * D + m * 128:k * D + (m + 1) * 128],
                        xn_fm.t[:, k * T + tc * 512:k * T + (tc + 1) * 512],
                        start=(k == 0), stop=(k == 7)),
                        reads=[wbf, xn_fm], writes=[psq[0]])
                p.op("scalar", lambda e, m=m, tc=tc: e.activation(
                    qk_fm.t[:, m * T + tc * 512:m * T + (tc + 1) * 512], psq[0].t[:, :],
                    AF.Identity, bias=qkb_sb.t[:, m:m + 1]),
                    reads=[psq[0], qkb_sb], writes=[qk_fm])
        load_w(vwT)
        for tt in range(NT):
            for nh in range(2):
                for k in range(8):
                    p.op("tensor", lambda e, tt=tt, nh=nh, k=k: e.matmul(
                        psq[nh].t[:, :], xn_fm.t[:, k * T + tt * 128:k * T + tt * 128 + 128],
                        wbf.t[:, k * D + nh * 512:k * D + (nh + 1) * 512],
                        start=(k == 0), stop=(k == 7)),
                        reads=[xn_fm, wbf], writes=[psq[nh]])
                p.op("vector", lambda e, tt=tt, nh=nh: e.tensor_add(
                    scratch.t[:, nh * 512:(nh + 1) * 512], psq[nh].t[:, :],
                    bvrep.t[:, nh * 512:(nh + 1) * 512]),
                    reads=[psq[nh], bvrep], writes=[scratch])
            p.op("vector", lambda e, tt=tt: e.tensor_copy(v_tm.t[:, tt * D:(tt + 1) * D], scratch.t[:, :]),
                 reads=[scratch], writes=[v_tm])
        for tt in range(NT):
            for k in range(8):
                p.op("tensor", lambda e, tt=tt, k=k: e.transpose(
                    pstb.t[:, :], qk_fm.t[:, k * T + tt * 128:k * T + tt * 128 + 128], identbf.t[:, :]),
                    reads=[qk_fm, identbf], writes=[pstb])
                p.op("scalar", lambda e, k=k: e.copy(xnb.t[:, k * 128:(k + 1) * 128], pstb.t[:, :]),
                     reads=[pstb], writes=[xnb])
            p.op("scalar", lambda e: e.activation(scratch.t[:, :], xnb.t[:, :], AF.Square),
                 reads=[xnb], writes=[scratch])
            p.op("vector", lambda e, tt=tt: e.reduce_sum(
                norms2.t[:, tt * NH:(tt + 1) * NH],
                scratch.t[:, :].rearrange("p (h dh) -> p h dh", dh=64), axis=AX.X),
                reads=[scratch], writes=[norms2])
            p.op("vector", lambda e: e.tensor_tensor(scratch.t[:, :], xnb.t[:, :], arep.t[:, :], ALU.mult),
                 reads=[xnb, arep], writes=[scratch])
            p.op("vector", lambda e, tt=tt: e.reduce_sum(
                qka.t[:, tt * NH:(tt + 1) * NH],
                scratch.t[:, :].rearrange("p (h dh) -> p h dh", dh=64), axis=AX.X),
                reads=[scratch], writes=[qka])
        p.op("vector", lambda e: e.tensor_copy(mx.t[:, :], norms2.t[:, 0:NH]),
             reads=[norms2], writes=[mx])
        for tt in range(1, NT):
            p.op("vector", lambda e, tt=tt: e.tensor_max(
                mx.t[:, :], mx.t[:, :], norms2.t[:, tt * NH:(tt + 1) * NH]),
                reads=[norms2, mx], writes=[mx])
        p.op("tensor", lambda e: e.transpose(pstf.t[0:16, 0:128], mx.t[:, :], identf.t[:, :]),
             reads=[mx, identf], writes=[pstf])
        p.op("vector", lambda e: e.reduce_max(mncol.t[:, 0:1], pstf.t[0:16, 0:128], axis=AX.X),
             reads=[pstf], writes=[mncol])
        p.op("scalar", lambda e: e.activation(mncol.t[:, 1:2], mncol.t[:, 0:1], AF.Sqrt),
             reads=[mncol], writes=[mncol])
        p.op("vector", lambda e: e.tensor_single_scalar(mncol.t[:, 1:2], mncol.t[:, 1:2], 1e-12, ALU.max),
             reads=[mncol], writes=[mncol])
        p.op("vector", lambda e: e.reciprocal(mncol.t[:, 2:3], mncol.t[:, 1:2]),
             reads=[mncol], writes=[mncol])
        p.op("vector", lambda e: e.tensor_scalar(mncol.t[:, 2:3], mncol.t[:, 2:3], 0.75, None, ALU.mult),
             reads=[mncol], writes=[mncol])
        p.op("vector", lambda e: e.tensor_tensor(mncol.t[:, 3:4], mncol.t[:, 2:3], mncol.t[:, 2:3], ALU.mult),
             reads=[mncol], writes=[mncol])
        p.op("tensor", lambda e: e.transpose(pstf.t[0:4, 0:16], mncol.t[:, :], identf.t[0:16, 0:16]),
             reads=[mncol, identf], writes=[pstf])
        p.op("vector", lambda e: e.tensor_copy(mncol.t[0:4, 0:16] if False else cbuf.t[0:4, 0:16], pstf.t[0:4, 0:16]),
             reads=[pstf], writes=[cbuf])
        p.dma(row.t[0:1, 0:16], cbuf.t[2:3, 0:16], row, cbuf)
        p.dma(row.t[0:1, 16:32], cbuf.t[3:4, 0:16], row, cbuf)
        emit_bcast(p, srep, 0, row, lambda o, n: row.t[0:1, o:o + n], 2 * NH, consts, psbc)
        for tt in range(NT):
            n2t = norms2.t[:, tt * NH:(tt + 1) * NH]
            qkat = qka.t[:, tt * NH:(tt + 1) * NH]
            st8 = stats.t[:, 0:NH]
            p.op("vector", lambda e, tt=tt, qkat=qkat: e.tensor_single_scalar(
                sqpm.t[:, tt * NH:(tt + 1) * NH], qkat, 0.0, ALU.is_ge),
                reads=[qka], writes=[sqpm])
            p.op("vector", lambda e, tt=tt: e.tensor_scalar(
                sqpm.t[:, tt * NH:(tt + 1) * NH], sqpm.t[:, tt * NH:(tt + 1) * NH], 2.0, -1.0,
                ALU.mult, ALU.add),
                reads=[sqpm], writes=[sqpm])
            p.op("vector", lambda e, n2t=n2t: e.tensor_tensor(st8, n2t, srep.t[:, NH:2 * NH], ALU.mult),
                 reads=[norms2, srep], writes=[stats])
            p.op("vector", lambda e, qkat=qkat: e.tensor_tensor(
                scratch.t[:, 0:NH], qkat, srep.t[:, 0:NH], ALU.mult),
                reads=[qka, srep], writes=[scratch])
            p.op("vector", lambda e: e.tensor_scalar(
                stats.t[:, 16:16 + NH] if False else scratch.t[:, NH:2 * NH],
                st8, 0.5, -1.0, ALU.subtract, ALU.mult),
                reads=[stats], writes=[scratch])
            p.op("vector", lambda e: e.scalar_tensor_tensor(
                scratch.t[:, 0:NH], scratch.t[:, NH:2 * NH], a65_sb.t[:, 0:1],
                scratch.t[:, 0:NH], ALU.mult, ALU.add),
                reads=[a65_sb, scratch], writes=[scratch])
            p.op("vector", lambda e: e.tensor_tensor(st8, st8, st8, ALU.mult),
                 reads=[stats], writes=[stats])
            p.op("vector", lambda e: e.tensor_scalar(
                scratch.t[:, NH:2 * NH], st8, 0.5, -1.0, ALU.subtract, ALU.mult),
                reads=[stats], writes=[scratch])
            p.op("vector", lambda e: e.scalar_tensor_tensor(
                scratch.t[:, 0:NH], scratch.t[:, NH:2 * NH], a65_sb.t[:, 1:2],
                scratch.t[:, 0:NH], ALU.mult, ALU.add),
                reads=[a65_sb, scratch], writes=[scratch])
            p.op("vector", lambda e, tt=tt: e.tensor_single_scalar(
                skpm.t[:, tt * NH:(tt + 1) * NH], scratch.t[:, 0:NH], 0.0, ALU.is_ge),
                reads=[scratch], writes=[skpm])
            p.op("vector", lambda e, tt=tt: e.tensor_scalar(
                skpm.t[:, tt * NH:(tt + 1) * NH], skpm.t[:, tt * NH:(tt + 1) * NH], 2.0, -1.0,
                ALU.mult, ALU.add),
                reads=[skpm], writes=[skpm])
            for src, off in ((sqpm, 0), (skpm, T)):
                p.op("tensor", lambda e, tt=tt, src=src: e.transpose(
                    pstf.t[0:16, 0:128], src.t[:, tt * NH:(tt + 1) * NH], identf.t[:, :]),
                    reads=[src, identf], writes=[pstf])
                p.op("scalar", lambda e, tt=tt, off=off: e.copy(
                    hrow.t[:, off + tt * 128:off + (tt + 1) * 128], pstf.t[0:16, 0:128]),
                    reads=[pstf], writes=[hrow])
        for h in range(NH):
            ft, po = h // 2, (h % 2) * 64
            c125 = p.cc(0.125)
            p.op("scalar", lambda e, ft=ft, po=po: e.activation(
                qpack.t[0:64, :], qk_fm.t[:, ft * T:(ft + 1) * T][po:po + 64, :], AF.Identity,
                scale=c125[0:64]),
                reads=[qk_fm, p.constbuf], writes=[qpack])
            p.dma(kpack.t[0:64, :], qk_fm.t[:, ft * T:(ft + 1) * T][po:po + 64, :], kpack, qk_fm)
            c5k = p.cc(5000.0)
            p.dma(row.t[0:1, 0:T], hrow.t[h:h + 1, 0:T], row, hrow)
            p.op("scalar", lambda e: e.activation(
                rowb.t[0:1, :], row.t[0:1, 0:T], AF.Identity, scale=c5k[0:1]),
                reads=[row, p.constbuf], writes=[rowb])
            p.dma(qpack.t[64:65, :], rowb.t[0:1, :], qpack, rowb)
            p.op("vector", lambda e: e.memset(rowb.t[0:1, :], -5000.0), writes=[rowb])
            p.dma(qpack.t[65:66, :], rowb.t[0:1, :], qpack, rowb)
            p.dma(row.t[0:1, 0:T], hrow.t[h:h + 1, T:2 * T], row, hrow)
            p.op("scalar", lambda e: e.copy(rowb.t[0:1, :], row.t[0:1, 0:T]),
                 reads=[row], writes=[rowb])
            p.dma(kpack.t[64:65, :], rowb.t[0:1, :], kpack, rowb)
            p.op("vector", lambda e: e.memset(rowb.t[0:1, :], 1.0), writes=[rowb])
            p.dma(kpack.t[65:66, :], rowb.t[0:1, :], kpack, rowb)
            for ic in range(2):
                i0 = ic * 1024
                for jt in range(NT):
                    for nh in range(2):
                        p.op("tensor", lambda e, jt=jt, nh=nh, i0=i0: e.matmul(
                            psq[nh].t[:, :], kpack.t[0:66, jt * 128:(jt + 1) * 128],
                            qpack.t[0:66, i0 + nh * 512:i0 + (nh + 1) * 512],
                            start=True, stop=True),
                            reads=[kpack, qpack], writes=[psq[nh]])
                        p.op("scalar", lambda e, jt=jt, nh=nh: e.activation(
                            e_sb.t[:, jt * 1024 + nh * 512:jt * 1024 + (nh + 1) * 512],
                            psq[nh].t[:, :], AF.Exp),
                            reads=[psq[nh]], writes=[e_sb])
                for nh in range(2):
                    for jt in range(NT):
                        p.op("tensor", lambda e, jt=jt, nh=nh: e.matmul(
                            pss.t[:, :], onesb.t[:, :],
                            e_sb.t[:, jt * 1024 + nh * 512:jt * 1024 + (nh + 1) * 512],
                            start=(jt == 0), stop=(jt == NT - 1)),
                            reads=[onesb, e_sb], writes=[pss])
                    p.op("vector", lambda e, nh=nh: e.tensor_copy(
                        row.t[0:1, nh * 512:(nh + 1) * 512], pss.t[:, :]),
                        reads=[pss], writes=[row])
                p.op("vector", lambda e: e.reciprocal(row.t[0:1, 0:1024], row.t[0:1, 0:1024]),
                     reads=[row], writes=[row])
                emit_bcast(p, rrep, 0, row, lambda o, n: row.t[0:1, o:o + n], 1024, consts, psbc)
                for jt in range(NT):
                    p.op("vector", lambda e, jt=jt: e.tensor_tensor(
                        e_sb.t[:, jt * 1024:(jt + 1) * 1024], e_sb.t[:, jt * 1024:(jt + 1) * 1024],
                        rrep.t[:, :], ALU.mult),
                        reads=[rrep, e_sb], writes=[e_sb])
                for m8 in range(8):
                    it = ic * 8 + m8
                    for jt in range(NT):
                        p.op("tensor", lambda e, jt=jt, m8=m8, h=h: e.matmul(
                            psctx.t[:, :], e_sb.t[:, jt * 1024 + m8 * 128:jt * 1024 + (m8 + 1) * 128],
                            v_tm.t[:, jt * D + h * 64:jt * D + (h + 1) * 64],
                            start=(jt == 0), stop=(jt == NT - 1)),
                            reads=[e_sb, v_tm], writes=[psctx])
                    p.op("vector", lambda e: e.tensor_copy(cbuf.t[:, :], psctx.t[:, :]),
                         reads=[psctx], writes=[cbuf])
                    p.dma(ctx_hbm.t[it, :, h * 64:(h + 1) * 64], cbuf.t[:, :], ctx_hbm, cbuf)
        for tt in range(NT):
            t0 = tt * 128
            p.dma(xt.t[:, :], x_hbm.t[b, t0:t0 + 128, :], xt, x_hbm)
            p.dma(scratch.t[:, :], ctx_hbm.t[tt, :, :], scratch, ctx_hbm)
            p.op("vector", lambda e: e.tensor_add(scratch.t[:, :], scratch.t[:, :], xt.t[:, :]),
                 reads=[xt, scratch], writes=[scratch])
            p.dma(y_hbm.t[b, t0:t0 + 128, :], scratch.t[:, :], y_hbm, scratch)


def launch_l2_attn(x_np, ins):
    def bld():
        p = P(_mk_nc())
        x_hbm = p.dram("x", [B_LOC, T, D], F32, "ExternalInput")
        y_hbm = p.dram("y", [B_LOC, T, D], F32, "ExternalOutput")
        consts = make_consts(p)
        ctx_hbm = p.dram("l2a_ctx", [T // 128, 128, D], F32, "Internal")
        build_l2_attn(p, x_hbm, y_hbm, consts, ctx_hbm)
        return p.build()
    nc = _cached("l2attn", bld)
    ha = np.asarray(ins["L2_attn_hash_a"], np.float32)
    io = {
        "l2a_qkwT": np.asarray(ins["L2_attn_qk_w"]).T,
        "l2a_qkb_fm": np.asarray(ins["L2_attn_qk_b"]).reshape(8, 128).T,
        "l2a_vwT": np.asarray(ins["L2_attn_v_w"]).T,
        "l2a_vb": ins["L2_attn_v_b"],
        "l2a_arow": np.tile(ha[:64], 16),
        "l2a_a6465": np.tile(ha[64:66][None, :], (128, 1)),
    }
    io = {k: np.ascontiguousarray(v, dtype=np.float32) for k, v in io.items()}
    eye = _np_bf16_eye()
    in_maps = [dict(io, x=x_np[c], eye128=eye) for c in range(NCORES)]
    res = run_bass_kernel_spmd(nc, in_maps, core_ids=list(range(NCORES)))
    return [r["y"] for r in res.results]



_NC_CACHE = {}


def _mk_nc():
    return bass.Bass("TRN2", target_bir_lowering=False)


def _cached(name, builder):
    if name not in _NC_CACHE:
        _NC_CACHE[name] = builder()
    return _NC_CACHE[name]


def _np_bf16_eye():
    return np.eye(128, dtype=np.float32)


def make_consts(p):
    eye = p.dram("eye128", [128, 128], F32, "ExternalInput")
    identf = p.sb([128, 128], F32, "identf")
    identbf = p.sb([128, 128], BF16, "identbf")
    p.dma(identf.t[:, :], eye.t[:, :], identf, eye)
    p.op("vector", lambda e: e.tensor_copy(identbf.t[:, :], identf.t[:, :]),
         reads=[identf], writes=[identbf])
    ones1 = p.sb([1, 128], F32, "ones1")
    p.op("vector", lambda e: e.memset(ones1.t[:, :], 1.0), writes=[ones1])
    return {"identf": identf, "identbf": identbf, "ones1": ones1}


def launch_l3_ffn(x_np, ins):
    def bld():
        p = P(_mk_nc())
        x_hbm = p.dram("x", [B_LOC, T, D], F32, "ExternalInput")
        y_hbm = p.dram("y", [B_LOC, T, D], F32, "ExternalOutput")
        consts = make_consts(p)
        build_l3_ffn(p, x_hbm, y_hbm, consts)
        return p.build()
    nc = _cached("l3ffn", bld)
    wgT = np.ascontiguousarray(ins["L3_ffn_w_gate"].T)
    wuT = np.ascontiguousarray(ins["L3_ffn_w_up"].T)
    wdT = np.ascontiguousarray(ins["L3_ffn_w_down"].T)
    eye = _np_bf16_eye()
    in_maps = [{"x": x_np[c], "eye128": eye,
                "l3f_wgT": wgT, "l3f_wuT": wuT, "l3f_wdT": wdT}
               for c in range(NCORES)]
    res = run_bass_kernel_spmd(nc, in_maps, core_ids=list(range(NCORES)))
    return [r["y"] for r in res.results]


def launch_moe(x_np, ins, which):
    def bld():
        p = P(_mk_nc())
        x_hbm = p.dram("x", [B_LOC, T, D], F32, "ExternalInput")
        y_hbm = p.dram("y", [B_LOC, T, D], F32, "ExternalOutput")
        consts = make_consts(p)
        build_moe_ffn(p, x_hbm, y_hbm, consts, which)
        return p.build()
    nc = _cached("moe_" + which, bld)
    eye = _np_bf16_eye()
    if which == "l2":
        io = {
            "l2f_gate_w": ins["L2_ffn_gate_w"],
            "l2f_gate_b": ins["L2_ffn_gate_b"],
            "l2f_w1": ins["L2_ffn_w1"],
            "l2f_b1r": np.ascontiguousarray(ins["L2_ffn_b1"].reshape(8, 32, 128).transpose(0, 2, 1)),
            "l2f_w2": ins["L2_ffn_w2"],
            "l2f_b2": ins["L2_ffn_b2"],
        }
    else:
        io = {
            "l1f_gate_wT": np.ascontiguousarray(ins["L1_ffn_gate_w"].T),
            "l1f_up_w": ins["L1_ffn_up_w"],
            "l1f_up_br": np.ascontiguousarray(ins["L1_ffn_up_b"].reshape(8, 8, 128).transpose(0, 2, 1)),
            "l1f_down_w": ins["L1_ffn_down_w"],
            "l1f_down_b": ins["L1_ffn_down_b"],
            "l1f_sh_up_w": ins["L1_ffn_sh_up_w"],
            "l1f_sh_up_br": np.ascontiguousarray(ins["L1_ffn_sh_up_b"].reshape(8, 128).T),
            "l1f_sh_down_w": ins["L1_ffn_sh_down_w"],
            "l1f_sh_down_b": ins["L1_ffn_sh_down_b"],
        }
    io = {k: np.ascontiguousarray(v) for k, v in io.items()}
    in_maps = [dict(io, x=x_np[c], eye128=eye) for c in range(NCORES)]
    res = run_bass_kernel_spmd(nc, in_maps, core_ids=list(range(NCORES)))
    return [r["y"] for r in res.results]


def _bw_vecs(w):
    wa = np.empty((12, 4096), np.float32)
    wb = np.empty_like(wa)
    f = np.arange(4096)
    for s_ in range(12):
        sg = 1 << s_
        k = (f // (2 * sg)) * sg + (f % sg)
        bit = (f // sg) % 2
        wa[s_] = w[s_, k, bit, bit]
        wb[s_] = w[s_, k, bit, 1 - bit]
    return wa, wb


def l0f_host_inputs(ins):
    io = {}
    for w, key in (("up", "L0_ffn_up_weights"), ("dn", "L0_ffn_down_weights")):
        wa, wb = _bw_vecs(np.asarray(ins[key], np.float32))
        wbx = np.empty((10, 4096), np.float32)
        f = np.arange(4096)
        for s_ in range(10):
            wbx[s_] = wb[s_][f ^ (1 << s_)]
        io[f"l0f_wa_{w}"] = wa[:10].reshape(10, 4, 1024)
        io[f"l0f_wbx_{w}"] = wbx.reshape(10, 4, 1024)
        io[f"l0f_st_{w}"] = np.stack([wa[10].reshape(32, 128).T, wb[10].reshape(32, 128).T,
                                      wa[11].reshape(32, 128).T, wb[11].reshape(32, 128).T])
    io["l0f_w_exp"] = ins["L0_ffn_W_exp"]
    io["l0f_w_con"] = ins["L0_ffn_W_con"]
    io["l0f_bexp_fm"] = np.asarray(ins["L0_ffn_b_exp"]).reshape(32, 128).T
    io["l0f_upb_fm"] = np.asarray(ins["L0_ffn_up_bias"]).reshape(32, 128).T
    io["l0f_bcon"] = ins["L0_ffn_b_con"]
    io["l0f_dnb"] = ins["L0_ffn_down_bias"]
    return {k: np.ascontiguousarray(v, dtype=np.float32) for k, v in io.items()}


def launch_l0_ffn(x_np, ins):
    def bld():
        p = P(_mk_nc())
        x_hbm = p.dram("x", [B_LOC, T, D], F32, "ExternalInput")
        y_hbm = p.dram("y", [B_LOC, T, D], F32, "ExternalOutput")
        consts = make_consts(p)
        mt_hbm = {w: p.dram(f"l0f_mt_{w}", [4, 8, 128, 1024], BF16, "Internal") for w in ("up", "dn")}
        build_l0_ffn(p, x_hbm, y_hbm, consts, mt_hbm)
        return p.build()
    nc = _cached("l0ffn", bld)
    io = l0f_host_inputs(ins)
    eye = _np_bf16_eye()
    in_maps = [dict(io, x=x_np[c], eye128=eye) for c in range(NCORES)]
    res = run_bass_kernel_spmd(nc, in_maps, core_ids=list(range(NCORES)))
    return [r["y"] for r in res.results]


def launch_l1_attn(x_np, ins):
    def bld():
        p = P(_mk_nc())
        x_hbm = p.dram("x", [B_LOC, T, D], F32, "ExternalInput")
        y_hbm = p.dram("y", [B_LOC, T, D], F32, "ExternalOutput")
        consts = make_consts(p)
        build_l1_attn(p, x_hbm, y_hbm, consts)
        return p.build()
    nc = _cached("l1attn", bld)
    io = {
        "l1a_wq": ins["L1_attn_Wq"], "l1a_wk": ins["L1_attn_Wk"], "l1a_wv": ins["L1_attn_Wv"],
        "l1a_bq": np.asarray(ins["L1_attn_bq"]).reshape(8, 128).T,
        "l1a_bk": np.asarray(ins["L1_attn_bk"]).reshape(8, 128).T,
        "l1a_bv": ins["L1_attn_bv"],
    }
    io = {k: np.ascontiguousarray(v, dtype=np.float32) for k, v in io.items()}
    eye = _np_bf16_eye()
    in_maps = [dict(io, x=x_np[c], eye128=eye) for c in range(NCORES)]
    res = run_bass_kernel_spmd(nc, in_maps, core_ids=list(range(NCORES)))
    return [r["y"] for r in res.results]


def kernel(**inputs):
    x = np.asarray(inputs["L0_attn_x"], np.float32)
    xs = [np.ascontiguousarray(x[2 * c:2 * c + 2]) for c in range(NCORES)]
    xs = launch_l0_ffn(xs, inputs)
    xs = launch_l0_attn(xs, inputs)
    xs = launch_l0_ffn(xs, inputs)
    xs = launch_moe(xs, inputs, "l1")
    xs = launch_l1_attn(xs, inputs)
    xs = launch_moe(xs, inputs, "l1")
    xs = launch_moe(xs, inputs, "l2")
    xs = launch_l2_attn(xs, inputs)
    xs = launch_moe(xs, inputs, "l2")
    xs = launch_l3_ffn(xs, inputs)
    xs = launch_l3_attn(xs, inputs)
    xs = launch_l3_ffn(xs, inputs)
    return np.concatenate(xs, axis=0).astype(np.float32)
